# version 09 (final)
# baseline (speedup 1.0000x reference)
import contextlib
import numpy as np
import ml_dtypes
import concourse.bass as bass
import concourse.mybir as mybir
from concourse.bass_utils import run_bass_kernel_spmd

F32 = mybir.dt.float32
BF16 = mybir.dt.bfloat16
I32 = mybir.dt.int32
AF = mybir.ActivationFunctionType
ALU = mybir.AluOpType
AX = mybir.AxisListType

T = 2048
D = 1024
NTT = 16
NDT = 8

L1_SKIP_FFN = False
DBG = None
ENGMAP = {'t': 'tensor', 'v': 'vector', 's': 'scalar', 'g': 'gpsimd', 'y': 'sync'}


class KB:

    def __init__(self):
        self.nc = bass.Bass("TRN2", target_bir_lowering=False)
        self.es = contextlib.ExitStack()
        self.sems = {}
        self.n = {}
        self.block = None
        self.waited = None
        self.inputs = {}
        self.outputs = {}

    def dram_in(self, name, shape, dt):
        h = self.nc.dram_tensor(name, list(shape), dt, kind="ExternalInput")
        self.inputs[name] = h
        return h

    def dram_out(self, name, shape, dt):
        h = self.nc.dram_tensor(name, list(shape), dt, kind="ExternalOutput")
        self.outputs[name] = h
        return h

    def sbuf(self, name, shape, dt, stack=None):
        return (stack or self.es).enter_context(self.nc.sbuf_tensor(name, list(shape), dt))

    def psum(self, name, shape, dt, stack=None):
        return (stack or self.es).enter_context(self.nc.psum_tensor(name, list(shape), dt))

    def start(self):
        for nm in ('t', 'v', 's', 'g', 'y', 'dg', 'dy'):
            self.sems[nm] = self.es.enter_context(self.nc.semaphore(nm))
            self.n[nm] = 0
        self.waited = {e: {k: 0 for k in self.sems} for e in ENGMAP}
        self.block = self.es.enter_context(self.nc.Block())

    def op(self, eng, deps, fn, dma=False):
        semname = {'g': 'dg', 'y': 'dy', 's': 'dy'}[eng] if dma else eng
        inc = 16 if dma else 1
        dl = []
        w = self.waited[eng]
        for (s, v) in deps:
            assert v <= self.n[s], f"dep on future: {s} {v} > {self.n[s]}"
            if v > w[s]:
                w[s] = v
                dl.append((self.sems[s], v))
        self.n[semname] += inc
        target = self.n[semname]
        sem = self.sems[semname]

        def build(E):
            for (sh, v) in dl:
                E.wait_ge(sh, v)
            fn(E).then_inc(sem, inc)

        getattr(self.block, ENGMAP[eng])(build)
        return (semname, target)

    def snap(self):
        return [(k, v) for k, v in self.n.items() if v > 0]

    def wait_only(self, eng, deps):
        dl = []
        w = self.waited[eng]
        for (s, v) in deps:
            assert v <= self.n[s]
            if v > w[s]:
                w[s] = v
                dl.append((self.sems[s], v))
        if not dl:
            return

        def build(E):
            for (sh, v) in dl:
                E.wait_ge(sh, v)
        getattr(self.block, ENGMAP[eng])(build)


def mm_chain(K, bank, mms, deps):
    h = None
    for i, fn in enumerate(mms):
        d = (deps + bank['ready']) if i == 0 else []
        st, sp = (i == 0), (i == len(mms) - 1)
        h = K.op('t', d, (lambda fn=fn, st=st, sp=sp: (lambda t: fn(t, st, sp)))())
    bank['wrote'] = h
    bank['ready'] = []
    return h


def bank_read(K, eng, bank, deps, fn):
    h = K.op(eng, deps + [bank['wrote']], fn)
    bank['ready'].append(h)
    return h



def build_program(layers=(0, 1, 2, 3), taps=False):
    K = KB()
    nc = K.nc

    xin = K.dram_in("x", [T, D], F32)
    out = K.dram_out("out", [T, D], F32)

    W = {}

    def din(name, shape, dt=F32):
        W[name] = K.dram_in(name, shape, dt)
        return W[name]

    din("ident", [128, 128], BF16)
    din("ones_row", [1, 128], BF16)
    din("ones_row32", [1, 128], F32)
    din("cvec", [1, 16], F32)
    if 0 in layers:
        din("L0_WqT", [D, D]); din("L0_WkT", [D, D]); din("L0_WvT", [D, D]); din("L0_WoT", [D, D])
        din("L0_bq", [D]); din("L0_bk", [D]); din("L0_bv", [D]); din("L0_bo", [D])
        din("L0_J", [16, 64, 64])
        din("L0_lam", [1, 1])
        din("L0_W1T", [D, 4096]); din("L0_b1", [4096])
        din("L0_W2T", [4096, D]); din("L0_b2", [D])
        din("L0_scale", [1, 1])
    if 1 in layers:
        din("L1_Wq", [D, D]); din("L1_Wk", [D, 256]); din("L1_Wv", [D, 256]); din("L1_Wo", [D, D])
        din("L1_cosq", [T, 512]); din("L1_sinq", [T, 512])
        din("L1_cosk", [T, 32]); din("L1_sink", [T, 32])
        din("L1_slot_init", [64, D]); din("L1_slot_scale", [64, 1])
        din("L1_w1aT", [D, 512]); din("L1_w1bT", [D, 512]); din("L1_b1", [512])
        din("L1_w2T", [512, 64]); din("L1_b2", [64])
        din("L1_wihT", [D, 3072]); din("L1_whhT", [D, 3072])
        din("L1_bih", [3072]); din("L1_bhh", [3072])
        din("L1_wvT", [D, D]); din("L1_bv", [D])
        din("L1_woT", [D, D]); din("L1_bo", [D])
        din("L1_tau", [1, 1])
    if 2 in layers:
        din("L2_W_in", [D, 4096]); din("L2_conv_w", [2048, 4])
        din("L2_W_x", [2048, 32]); din("L2_W_dt", [16, 2048]); din("L2_b_dt", [2048])
        din("L2_W_Bp", [16, 2048]); din("L2_D", [2048]); din("L2_W_out", [2048, D])
        din("L2_Wg", [D, 8])
        din("L2_W1", [8, D, 1408]); din("L2_W2", [8, D, 1408]); din("L2_W3", [8, 1408, D])
        din("L2_Ws1", [D, 2816]); din("L2_Ws2", [D, 2816]); din("L2_Ws3", [2816, D])
    if 3 in layers:
        din("L3_WqT", [D, D]); din("L3_WkT", [D, D]); din("L3_WvT", [D, D]); din("L3_WoT", [D, D])
        din("L3_mask_diag", [128, 128], BF16)
        din("L3_mask_head", [128, 128], BF16)
        din("L3_gamma", [D]); din("L3_beta", [D])
        din("L3_fc1_q", [D, 4096], I32); din("L3_fc1_scales", [8, 4096]); din("L3_fc1_bias", [4096])
        din("L3_fc2_q", [4096, D], I32); din("L3_fc2_scales", [32, D]); din("L3_fc2_bias", [D])

    X = K.sbuf("X", [128, NTT * 1024], F32)
    XNT = K.sbuf("XNT", [128, NTT * 1024], BF16)
    XNF = K.sbuf("XNF", [128, NDT * 2048], BF16)
    IDENT = K.sbuf("IDENT", [128, 128], BF16)
    ONESR = K.sbuf("ONESR", [1, 128], BF16)
    ONESR32 = K.sbuf("ONESR32", [1, 128], F32)
    CVROW = K.sbuf("CVROW", [1, 16], F32)
    CVC = K.sbuf("CVC", [128, 16], F32)
    LN_ST = K.sbuf("LN_ST", [128, 16 * 8], F32)

    PB = []
    for i in range(6):
        PB.append({'ap': K.psum(f"pb{i}", [128, 512], F32), 'ready': [], 'wrote': None})
    QB = []
    for i in range(2):
        QB.append({'ap': K.psum(f"qb{i}", [128, 1024], BF16), 'ready': [], 'wrote': None})

    K.start()

    h_ident = K.op('y', [], lambda y: y.dma_start(IDENT[:], W["ident"][:, :]), dma=True)
    h_ones = K.op('y', [], lambda y: y.dma_start(ONESR[:], W["ones_row"][:, :]), dma=True)
    h_ones32 = K.op('y', [], lambda y: y.dma_start(ONESR32[:], W["ones_row32"][:, :]), dma=True)
    h_cv0 = K.op('y', [], lambda y: y.dma_start(CVROW[:], W["cvec"][:, :]), dma=True)

    def bcast_row(dst, srcrow, n, deps, bank, scale_col=None, f32=True):
        ones = ONESR32 if f32 else ONESR
        hs = []
        for c0 in range(0, n, 512):
            w_ = min(512, n - c0)
            mm_chain(K, bank, [lambda t, st, sp, c0=c0, w_=w_: t.matmul(
                bank['ap'][:, 0:w_], ones[:, 0:128], srcrow[:, c0:c0 + w_],
                start=True, stop=True)], list(deps))
            if scale_col is None:
                hs.append(bank_read(K, 'v', bank, [], lambda v, c0=c0, w_=w_: v.tensor_copy(
                    dst[:, c0:c0 + w_], bank['ap'][:, 0:w_])))
            else:
                hs.append(bank_read(K, 'v', bank, [], lambda v, c0=c0, w_=w_: v.tensor_scalar_mul(
                    dst[:, c0:c0 + w_], bank['ap'][:, 0:w_], scale_col)))
        return hs[-1]

    h_cv = bcast_row(CVC, CVROW, 16, [h_cv0], PB[5])
    C_LNEPS, C_CLIPLO, C_WSEPS = (CVC[:, i:i + 1] for i in range(3))
    xld = []
    for tt in range(NTT):
        xld.append(K.op('y', [], lambda y, tt=tt: y.dma_start(
            X[:, tt * 1024:(tt + 1) * 1024], xin[tt * 128:(tt + 1) * 128, :]), dma=True))
    h_xload = xld[-1]

    def load_cols(stack, name, n, deps=()):
        t = K.sbuf(f"c_{name}", [128, n // 128], F32, stack)
        src = W[name].ap().rearrange("(c p) -> p c", p=128)
        h = K.op('y', list(deps), lambda y: y.dma_start(t[:], src, allow_slow_non_contiguous=True), dma=True)
        return t, h

    def load_row(stack, name, n, deps=(), dt=F32, cast=False):
        t = K.sbuf(f"r_{name}", [1, n], dt, stack)
        src = W[name].ap().rearrange("(a n) -> a n", a=1)
        eng = 'g' if cast else 'y'
        h = K.op(eng, list(deps), lambda q: q.dma_start(t[:], src), dma=True)
        return t, h

    def load_wT(stack, name, K_, N_, c0=0, ncols=None, deps=(), bufname=None, dst=None,
                kt0=0, nkt=None):
        ncols = ncols if ncols is not None else N_
        nkt = nkt if nkt is not None else K_ // 128
        t = dst if dst is not None else K.sbuf(bufname or f"w_{name}_{c0}", [128, nkt * ncols], BF16, stack)
        src = W[name].ap().rearrange("(kt p) n -> p kt n", p=128)[:, kt0:kt0 + nkt, c0:c0 + ncols]
        dv = t.ap().rearrange("p (kt n) -> p kt n", n=ncols)
        h = K.op('g', list(deps), lambda g: g.dma_start(dv, src), dma=True)
        return t, h

    def emit_ln(join):
        c_sum, c_ssq, c_mu, c_var, c_r, c_nmr = (LN_ST[:, 16 * i:16 * (i + 1)] for i in range(6))
        hs = []
        hv = []
        for tt in range(NTT):
            xt = X[:, tt * 1024:(tt + 1) * 1024]
            hs.append(K.op('s', join if tt == 0 else [], lambda s, tt=tt, xt=xt: s.activation(
                XNT[:, tt * 1024:(tt + 1) * 1024], xt, AF.Square, accum_out=c_ssq[:, tt:tt + 1])))
            hv.append(K.op('v', join if tt == 0 else [], lambda v, tt=tt, xt=xt: v.reduce_sum(
                c_sum[:, tt:tt + 1], xt, axis=AX.X)))
        K.op('v', [hs[-1]], lambda v: v.tensor_scalar_mul(c_mu[:], c_sum[:], 1.0 / 1024.0))
        K.op('v', [], lambda v: v.tensor_scalar_mul(c_var[:], c_ssq[:], 1.0 / 1024.0))
        K.op('v', [], lambda v: v.tensor_tensor(LN_ST[:, 96:112], c_mu[:], c_mu[:], ALU.mult))
        hvar = K.op('v', [], lambda v: v.tensor_sub(c_var[:], c_var[:], LN_ST[:, 96:112]))
        hstd = K.op('s', [hvar, h_cv], lambda s: s.activation(LN_ST[:, 112:128], c_var[:], AF.Sqrt, bias=C_LNEPS))
        hr = K.op('v', [hstd], lambda v: v.reciprocal(c_r[:], LN_ST[:, 112:128]))
        hnmr = K.op('v', [], lambda v: v.scalar_tensor_tensor(c_nmr[:], c_mu[:], -1.0, c_r[:], ALU.mult, ALU.mult))
        hxnt = []
        for tt in range(NTT):
            hxnt.append(K.op('s', [hr, hnmr] if tt == 0 else [], lambda s, tt=tt: s.activation(
                XNT[:, tt * 1024:(tt + 1) * 1024], X[:, tt * 1024:(tt + 1) * 1024],
                AF.Identity, bias=c_nmr[:, tt:tt + 1], scale=c_r[:, tt:tt + 1])))
        for dt in range(NDT):
            for g in range(2):
                b = QB[g]
                for j in range(8):
                    tt = g * 8 + j
                    d = ([hxnt[-1]] if (dt == 0 and g == 0 and j == 0) else []) + (b['ready'] if j == 0 else [])
                    h = K.op('t', d, lambda t, tt=tt, dt=dt, b=b, j=j: t.transpose(
                        b['ap'][:, j * 128:(j + 1) * 128],
                        XNT[:, tt * 1024 + dt * 128: tt * 1024 + (dt + 1) * 128], IDENT[:]))
                    if j == 7:
                        b['wrote'] = h
                        b['ready'] = []
                bank_read(K, 'v', b, [], lambda v, dt=dt, g=g, b=b: v.tensor_copy(
                    XNF[:, dt * 2048 + g * 1024: dt * 2048 + (g + 1) * 1024], b['ap'][:]))
        return K.snap()

    def x_add(bank, tt, oc, deps=()):
        return bank_read(K, 'v', bank, list(deps), lambda v: v.tensor_add(
            X[:, tt * 1024 + oc * 512: tt * 1024 + (oc + 1) * 512],
            X[:, tt * 1024 + oc * 512: tt * 1024 + (oc + 1) * 512],
            bank['ap'][:]))

    def emit_l0(join):
        ls = contextlib.ExitStack()
        xnt0 = K.sbuf("l0_xnt0", [128, 1024], BF16, ls)
        h_xnt0 = K.op('v', join, lambda v: v.tensor_copy(xnt0[:], XNT[:, 0:1024]))

        bq, h_bq = load_cols(ls, "L0_bq", D, join)
        bk, h_bk = load_cols(ls, "L0_bk", D, join)
        bvr, h_bv = load_row(ls, "L0_bv", D, join, dt=BF16, cast=True)
        bo, h_bo = load_row(ls, "L0_bo", D, join, dt=BF16, cast=True)
        lamt = K.sbuf("l0_lam", [1, 1], F32, ls)
        h_lam0 = K.op('y', join, lambda y: y.dma_start(lamt[:], W["L0_lam"][:, :]), dma=True)
        lamc = K.sbuf("l0_lamc", [128, 1], F32, ls)
        h_lam = bcast_row(lamc, lamt, 1, [h_lam0], PB[5])
        jsb = K.sbuf("l0_J", [128, 512], BF16, ls)
        hj = []
        jsrc = W["L0_J"].ap()
        for g in range(16):
            hj.append(K.op('g', join, lambda q, g=g: q.dma_start(
                jsb[64 * (g % 2):64 * (g % 2) + 64, (g // 2) * 64:(g // 2) * 64 + 64],
                jsrc[g, :, :]), dma=True))
        bq8 = K.sbuf("l0_bq8", [128, 8], F32, ls)
        h_bq8 = K.op('v', [h_bq], lambda v: v.tensor_scalar_mul(bq8[:], bq[:], 0.125))

        o_tm = XNT

        als = contextlib.ExitStack()
        vaug = K.sbuf("l0_vaug", [128, NTT * 130], BF16, als)
        h_vones = K.op('v', join, lambda v: v.memset(vaug[:], 1.0))

        wq_g = K.sbuf("l0_wqg", [128, 8 * 128], BF16, als)
        wk_g = K.sbuf("l0_wkg", [128, 8 * 128], BF16, als)
        wv_g = K.sbuf("l0_wvg", [128, 8 * 128], BF16, als)
        qs_g = K.sbuf("l0_qs", [128, 2048], BF16, als)
        tq_g = K.sbuf("l0_tq", [128, 2048], BF16, als)
        kk_g = K.sbuf("l0_kk", [128, 2048], BF16, als)
        tk_g = K.sbuf("l0_tk", [128, 2048], BF16, als)
        qj_g = K.sbuf("l0_qj", [128, 2048], BF16, als)
        PT = K.sbuf("l0_PT", [128, 16 * 512], BF16, als)
        rvq = K.sbuf("l0_rv", [128, 4], F32, als)

        wld = [None]
        h_vaug_last = [h_vones]
        for hg in range(8):
            c0 = 128 * hg
            gd = hg
            wdep = [] if wld[0] is None else [wld[0]]
            _, hq = load_wT(als, "L0_WqT", D, D, c0, 128, deps=join + wdep, dst=wq_g)
            _, hk = load_wT(als, "L0_WkT", D, D, c0, 128, deps=join + wdep, dst=wk_g)
            _, hv = load_wT(als, "L0_WvT", D, D, c0, 128, deps=join + wdep, dst=wv_g)

            for tc in range(4):
                b = PB[tc % 2]
                mms = []
                for kt in range(8):
                    mms.append(lambda t, st, sp, kt=kt, tc=tc, b=b: t.matmul(
                        b['ap'][:], wq_g[:, kt * 128:(kt + 1) * 128],
                        XNF[:, kt * 2048 + tc * 512: kt * 2048 + (tc + 1) * 512], start=st, stop=sp))
                mm_chain(K, b, mms, [hq, h_bq8])
                bank_read(K, 's', b, [], lambda s, tc=tc, b=b: s.activation(
                    qs_g[:, tc * 512:(tc + 1) * 512], b['ap'][:],
                    AF.Identity, bias=bq8[:, gd:gd + 1], scale=0.125))
                bank_read(K, 's', b, [], lambda s, tc=tc, b=b: s.activation(
                    tq_g[:, tc * 512:(tc + 1) * 512], b['ap'][:],
                    AF.Tanh, bias=bq[:, gd:gd + 1]))
                b2_ = PB[2 + tc % 2]
                mms = []
                for kt in range(8):
                    mms.append(lambda t, st, sp, kt=kt, tc=tc, b=b2_: t.matmul(
                        b['ap'][:], wk_g[:, kt * 128:(kt + 1) * 128],
                        XNF[:, kt * 2048 + tc * 512: kt * 2048 + (tc + 1) * 512], start=st, stop=sp))
                mm_chain(K, b2_, mms, [hk, h_bk])
                bank_read(K, 's', b2_, [], lambda s, tc=tc, b=b2_: s.activation(
                    kk_g[:, tc * 512:(tc + 1) * 512], b2_['ap'][:],
                    AF.Identity, bias=bk[:, gd:gd + 1]))
                bank_read(K, 's', b2_, [], lambda s, tc=tc, b=b2_: s.activation(
                    tk_g[:, tc * 512:(tc + 1) * 512], b2_['ap'][:],
                    AF.Tanh, bias=bk[:, gd:gd + 1]))

            for tt in range(NTT):
                b = PB[4 + tt % 2]
                mms = []
                for kt in range(8):
                    mms.append(lambda t, st, sp, kt=kt, tt=tt, b=b: t.matmul(
                        b['ap'][:, 0:128], XNF[:, kt * 2048 + tt * 128: kt * 2048 + (tt + 1) * 128],
                        wv_g[:, kt * 128:(kt + 1) * 128], start=st, stop=False))
                mms.append(lambda t, st, sp, tt=tt, b=b: t.matmul(
                    b['ap'][:, 0:128], ONESR[:, 0:128], bvr[:, c0:c0 + 128], start=False, stop=True))
                mm_chain(K, b, mms, [hv, h_bv, h_vaug_last[0]])
                dst = vaug[:, tt * 130:(tt + 1) * 130].rearrange("p (h c) -> p h c", c=65)[:, :, 0:64]
                src = b['ap'][:, 0:128].rearrange("p (h c) -> p h c", c=64)
                h = bank_read(K, 'v', b, [], lambda v, dst=dst, src=src: v.tensor_copy(dst, src))
                h_vaug_last[0] = h

            hqj = []
            for hl in range(2):
                head = 2 * hg + hl
                bse = 64 * hl
                for tc in range(4):
                    bb = PB[tc % 2]
                    mm_chain(K, bb, [lambda t, st, sp, head=head, bse=bse, tc=tc, bb=bb: t.matmul(
                        bb['ap'][bse:bse + 64, :], jsb[bse:bse + 64, (head // 2) * 64:(head // 2) * 64 + 64],
                        tq_g[bse:bse + 64, tc * 512:(tc + 1) * 512],
                        start=True, stop=True, tile_position=(bse, bse))], [h_lam, hj[-1]])
                    hqj.append(bank_read(K, 'v', bb, [], lambda v, bse=bse, tc=tc, bb=bb: v.tensor_scalar_mul(
                        qj_g[bse:bse + 64, tc * 512:(tc + 1) * 512],
                        bb['ap'][bse:bse + 64, :], lamc[bse:bse + 64, :])))

            for hl in range(2):
                head = 2 * hg + hl
                bse = 64 * hl
                for qc in range(4):
                    hexp = []
                    for kt in range(NTT):
                        b = PB[kt % 3]
                        mms = [
                            lambda t, st, sp, kt=kt, b=b: t.matmul(
                                b['ap'][:], kk_g[bse:bse + 64, kt * 128:(kt + 1) * 128],
                                qs_g[bse:bse + 64, qc * 512:(qc + 1) * 512],
                                start=True, stop=False, tile_position=(bse, 0)),
                            lambda t, st, sp, kt=kt, b=b: t.matmul(
                                b['ap'][:], tk_g[bse:bse + 64, kt * 128:(kt + 1) * 128],
                                qj_g[bse:bse + 64, qc * 512:(qc + 1) * 512],
                                start=False, stop=True, tile_position=(bse, 0)),
                        ]
                        mm_chain(K, b, mms, hqj[-1:] if kt == 0 else [])
                        hexp.append(bank_read(K, 's', b, [], lambda s, kt=kt, b=b: s.activation(
                            PT[:, kt * 512:(kt + 1) * 512], b['ap'][:], AF.Exp)))
                    b = PB[3]
                    h0 = None
                    first = True
                    for qt in range(4):
                        for kt in range(NTT):
                            d = (hexp + b['ready']) if first else []
                            first = False
                            h0 = K.op('t', d, lambda t, qt=qt, kt=kt, b=b, hl=hl: t.matmul(
                                b['ap'][:, qt * 65:(qt + 1) * 65],
                                PT[:, kt * 512 + qt * 128: kt * 512 + qt * 128 + 128],
                                vaug[:, kt * 130 + hl * 65: kt * 130 + (hl + 1) * 65],
                                start=(kt == 0), stop=(kt == NTT - 1)))
                    b['wrote'] = h0
                    b['ready'] = []
                    hrv = bank_read(K, 'v', b, [], lambda v, b=b: v.reciprocal(
                        rvq[:, 0:4], b['ap'][:, 0:260].rearrange("p (qt c) -> p qt c", c=65)[:, :, 64]))
                    for qt in range(4):
                        xd = [h_xnt0] if (hg == 0 and hl == 0 and qc == 0 and qt == 0) else []
                        bank_read(K, 'v', b, [hrv] + xd, lambda v, qt=qt, b=b, qc=qc, head=head: v.tensor_scalar_mul(
                            o_tm[:, (qc * 4 + qt) * 1024 + head * 64: (qc * 4 + qt) * 1024 + head * 64 + 64],
                            b['ap'][:, qt * 65: qt * 65 + 64], rvq[:, qt:qt + 1]))
            wld[0] = ('t', K.n['t'])
        als.close()

        ols = contextlib.ExitStack()
        o_fm = K.sbuf("l0_ofm", [128, NDT * 2048], BF16, ols)
        wo, hwo = load_wT(ols, "L0_WoT", D, D, deps=join + [('t', K.n['t'])])
        h_o_done = ('s', K.n['s'])
        hofm = []
        for dt in range(NDT):
            for g in range(2):
                b = QB[g]
                for j in range(8):
                    tt = g * 8 + j
                    d = (b['ready'] + [h_o_done]) if j == 0 else []
                    h = K.op('t', d, lambda t, tt=tt, dt=dt, b=b, j=j: t.transpose(
                        b['ap'][:, j * 128:(j + 1) * 128],
                        o_tm[:, tt * 1024 + dt * 128: tt * 1024 + (dt + 1) * 128], IDENT[:]))
                    if j == 7:
                        b['wrote'] = h
                        b['ready'] = []
                hofm.append(bank_read(K, 'v', b, [], lambda v, dt=dt, g=g, b=b: v.tensor_copy(
                    o_fm[:, dt * 2048 + g * 1024: dt * 2048 + (g + 1) * 1024], b['ap'][:])))

        for tt in range(NTT):
            for oc in range(2):
                b = PB[(tt * 2 + oc) % 4]
                mms = []
                for kt in range(8):
                    mms.append(lambda t, st, sp, kt=kt, tt=tt, oc=oc, b=b: t.matmul(
                        b['ap'][:], o_fm[:, kt * 2048 + tt * 128: kt * 2048 + (tt + 1) * 128],
                        wo[:, kt * 1024 + oc * 512: kt * 1024 + (oc + 1) * 512], start=st, stop=False))
                mms.append(lambda t, st, sp, oc=oc, b=b: t.matmul(
                    b['ap'][:], ONESR[:, 0:128], bo[:, oc * 512:(oc + 1) * 512], start=False, stop=True))
                mm_chain(K, b, mms, [hwo, h_bo, hofm[-1]] if (tt == 0 and oc == 0) else [])
                x_add(b, tt, oc)
        ols.close()

        fls = contextlib.ExitStack()
        join = join + [('t', K.n['t']), ('v', K.n['v'])]
        sct = K.sbuf("l0_sc", [1, 1], F32, fls)
        h_sc0 = K.op('y', join, lambda y: y.dma_start(sct[:], W["L0_scale"][:, :]), dma=True)
        scc = K.sbuf("l0_scc", [128, 1], F32, fls)
        h_sc = bcast_row(scc, sct, 1, [h_sc0], PB[5])
        ebc = K.sbuf("l0_ebc", [128, 8 * 1024], BF16, fls)
        est = K.sbuf("l0_est", [1, 1024], BF16, fls)
        heb = []
        hprev = h_xnt0
        for i in range(8):
            h_est = K.op('y', [h_xnt0, hprev], lambda y, i=i: y.dma_start(
                est[:], bass.AP(xnt0, (1 + 2 * i) * 1024, [[1024, 1], [1, 1024]])), dma=True)
            hprev = bcast_row(ebc[:, i * 1024:(i + 1) * 1024], est, 1024,
                              [h_est, h_sc], PB[5], scale_col=scc[:, 0:1], f32=False)
            heb.append(hprev)

        b1c, h_b1 = load_cols(fls, "L0_b1", 4096, join)
        b2r, h_b2 = load_row(fls, "L0_b2", D, join, dt=BF16, cast=True)
        w1q = K.sbuf("l0_w1q", [128, 8 * 1024], BF16, fls)
        w2q = K.sbuf("l0_w2q", [128, 8 * 1024], BF16, fls)
        hbuf = K.sbuf("l0_hbuf", [128, 8 * 256], BF16, fls)

        prev_use = []
        for hq in range(4):
            i1 = hq
            _, hw1 = load_wT(fls, "L0_W1T", D, 4096, hq * 1024, 1024, deps=join + prev_use, dst=w1q)
            _, hw2 = load_wT(fls, "L0_W2T", 4096, D, deps=join + prev_use, dst=w2q,
                             kt0=hq * 8, nkt=8)
            prev_use = []
            hmod = []
            for kt in range(8):
                oddcol = XNF[:, kt * 2048 + (2 + 2 * i1): kt * 2048 + (2 + 2 * i1) + 1]
                hmod.append(K.op('v', [hw1, heb[-1]] if kt == 0 else [], lambda v, kt=kt, oddcol=oddcol: v.scalar_tensor_tensor(
                    w1q[:, kt * 1024:(kt + 1) * 1024],
                    ebc[:, i1 * 1024:(i1 + 1) * 1024], oddcol,
                    w1q[:, kt * 1024:(kt + 1) * 1024], ALU.mult, ALU.add)))
            for ktl in range(8):
                for i in range(4):
                    oddcol = XNF[:, ktl * 2048 + (10 + 2 * i): ktl * 2048 + (10 + 2 * i) + 1]
                    ebs = ebc.ap().rearrange("p (i n) -> p i n", i=8)[:, 4 + i, hq::4]
                    hmod.append(K.op('v', [hw2] if (ktl == 0 and i == 0) else [], lambda v, ktl=ktl, i=i, ebs=ebs, oddcol=oddcol: v.scalar_tensor_tensor(
                        w2q[:, ktl * 1024 + i * 256: ktl * 1024 + (i + 1) * 256],
                        ebs, oddcol,
                        w2q[:, ktl * 1024 + i * 256: ktl * 1024 + (i + 1) * 256], ALU.mult, ALU.add)))

            for tb in range(8):
                hh_done = []
                for hc in range(8):
                    b = PB[hc % 2]
                    bcol = hq * 8 + hc
                    mms = []
                    for kt in range(8):
                        mms.append(lambda t, st, sp, kt=kt, hc=hc, tb=tb, b=b: t.matmul(
                            b['ap'][:, 0:256], w1q[:, kt * 1024 + hc * 128: kt * 1024 + (hc + 1) * 128],
                            XNF[:, kt * 2048 + tb * 256: kt * 2048 + (tb + 1) * 256], start=st, stop=sp))
                    mm_chain(K, b, mms, (hmod + [h_b1]) if (hc == 0 and tb == 0) else [])
                    hh_done.append(bank_read(K, 's', b, [], lambda s, hc=hc, b=b, bcol=bcol: s.activation(
                        hbuf[:, hc * 256:(hc + 1) * 256], b['ap'][:, 0:256],
                        AF.Gelu, bias=b1c[:, bcol:bcol + 1])))
                for ttb in range(2):
                    for oc in range(2):
                        b = PB[2 + (ttb * 2 + oc) % 2]
                        mms = []
                        for kt in range(8):
                            mms.append(lambda t, st, sp, kt=kt, ttb=ttb, oc=oc, b=b: t.matmul(
                                b['ap'][:], hbuf[:, kt * 256 + ttb * 128: kt * 256 + (ttb + 1) * 128],
                                w2q[:, kt * 1024 + oc * 512: kt * 1024 + (oc + 1) * 512], start=st, stop=False))
                        if hq == 0:
                            mms.append(lambda t, st, sp, oc=oc, b=b: t.matmul(
                                b['ap'][:], ONESR[:, 0:128], b2r[:, oc * 512:(oc + 1) * 512],
                                start=False, stop=True))
                        else:
                            mms[-1] = (lambda fn: lambda t, st, sp: fn(t, st, True))(mms[-1])
                        deps = hh_done[-1:] + ((hmod + [h_b2]) if (tb == 0 and ttb == 0 and oc == 0) else [])
                        mm_chain(K, b, mms, deps)
                        hu = x_add(b, tb * 2 + ttb, oc)
                        prev_use.append(hu)
        fls.close()
        ls.close()
        return K.snap()


    def maybe_dbg(name, buf, shape):
        if DBG != name:
            return
        dd = K.dram_out("dbg", shape, F32)
        hd = K.op('g', K.snap(), lambda g: g.dma_start(dd[:, :], buf[0:shape[0], 0:shape[1]]), dma=True)
        for e_ in ('t', 'v', 's', 'g', 'y'):
            K.wait_only(e_, [hd])

    def emit_l1(join):
        ls = contextlib.ExitStack()
        vaug = K.sbuf("l1_vaug", [128, NTT * 260], BF16, ls)
        h_vones = K.op('v', join, lambda v: v.memset(vaug[:], 1.0))
        k_both = K.sbuf("l1_kboth", [128, 4 * 2048], BF16, ls)
        rsc = K.sbuf("l1_rsc", [128, 512], F32, ls)
        kls = contextlib.ExitStack()
        krt = K.sbuf("l1_krt", [128, 256], BF16, kls)
        wk, hwk = load_wT(kls, "L1_Wk", D, 256, deps=join)
        wv, hwv = load_wT(kls, "L1_Wv", D, 256, deps=join)
        coskr = K.sbuf("l1_coskr", [128, NTT * 32], F32, kls)
        sinkr = K.sbuf("l1_sinkr", [128, NTT * 32], F32, kls)
        h_ck = K.op('y', join, lambda y: y.dma_start(
            coskr.ap().rearrange("p (tt f) -> p tt f", f=32),
            W["L1_cosk"].ap().rearrange("(tt p) f -> p tt f", p=128)), dma=True)
        h_sk = K.op('y', join, lambda y: y.dma_start(
            sinkr.ap().rearrange("p (tt f) -> p tt f", f=32),
            W["L1_sink"].ap().rearrange("(tt p) f -> p tt f", p=128)), dma=True)
        k_nat = K.sbuf("l1_knat", [128, 2 * 2048], BF16, kls)
        hkn = []
        for tt in range(NTT):
            b = PB[tt % 2]
            mms = []
            for kt in range(8):
                mms.append(lambda t, st, sp, kt=kt, tt=tt, b=b: t.matmul(
                    b['ap'][:, 0:256], XNF[:, kt * 2048 + tt * 128: kt * 2048 + (tt + 1) * 128],
                    wk[:, kt * 256:(kt + 1) * 256], start=st, stop=sp))
            mm_chain(K, b, mms, [hwk] if tt == 0 else [])
            ce = coskr[:, tt * 32:(tt + 1) * 32]
            se = sinkr[:, tt * 32:(tt + 1) * 32]
            for h4 in range(4):
                pe = b['ap'][:, h4 * 64:(h4 + 1) * 64].rearrange("p (f two) -> p f two", two=2)
                t1 = rsc[:, 0:32]; t2 = rsc[:, 32:64]
                kre = krt[:, h4 * 64:(h4 + 1) * 64].rearrange("p (f two) -> p f two", two=2)
                hv1 = bank_read(K, 'v', b, ([h_ck, h_sk] if (tt == 0 and h4 == 0) else []) + ([('t', K.n['t'])] if h4 == 0 else []),
                                lambda v, pe=pe, ce=ce, t1=t1: v.tensor_tensor(t1, pe[:, :, 0], ce, ALU.mult))
                K.op('v', [], lambda v, pe=pe, se=se, t2=t2: v.tensor_tensor(t2, pe[:, :, 1], se, ALU.mult))
                K.op('v', [], lambda v, kre=kre, t1=t1, t2=t2: v.tensor_sub(kre[:, :, 0], t1, t2))
                K.op('v', [], lambda v, pe=pe, se=se, t1=t1: v.tensor_tensor(t1, pe[:, :, 0], se, ALU.mult))
                K.op('v', [], lambda v, pe=pe, ce=ce, t2=t2: v.tensor_tensor(t2, pe[:, :, 1], ce, ALU.mult))
                hkr = K.op('v', [], lambda v, kre=kre, t1=t1, t2=t2: v.tensor_add(kre[:, :, 1], t1, t2))
            b['ready'].append(hkr)
            for half in range(2):
                qb = QB[half]
                hh = K.op('t', (qb['ready'] + [hkr]) if True else [], lambda t, tt=tt, half=half, qb=qb: t.transpose(
                    qb['ap'][:, (tt % 4) * 128:(tt % 4 + 1) * 128],
                    krt[:, half * 128:(half + 1) * 128], IDENT[:]))
                qb['wrote'] = hh
                qb['ready'] = []
                if tt % 4 == 3:
                    hkn.append(bank_read(K, 'v', qb, [], lambda v, tt=tt, half=half, qb=qb: v.tensor_copy(
                        k_nat[:, half * 2048 + (tt - 3) * 128: half * 2048 + (tt + 1) * 128],
                        qb['ap'][:, 0:512])))
            b2 = PB[2 + tt % 2]
            mms = []
            for kt in range(8):
                mms.append(lambda t, st, sp, kt=kt, tt=tt, b2=b2: t.matmul(
                    b2['ap'][:, 0:256], XNF[:, kt * 2048 + tt * 128: kt * 2048 + (tt + 1) * 128],
                    wv[:, kt * 256:(kt + 1) * 256], start=st, stop=sp))
            mm_chain(K, b2, mms, [hwv, h_vones] if tt == 0 else [])
            dst = vaug[:, tt * 260:(tt + 1) * 260].rearrange("p (h c) -> p h c", c=65)[:, :, 0:64]
            srcv = b2['ap'][:, 0:256].rearrange("p (h c) -> p h c", c=64)
            bank_read(K, 'v', b2, [], lambda v, dst=dst, srcv=srcv: v.tensor_copy(dst, srcv))
        hkb = []
        for kv in range(4):
            srcbase = 64 * (kv % 2)
            tile = kv // 2
            for dstbase in (0, 64):
                hkb.append(K.op('y', [hkn[-1]], lambda y, kv=kv, srcbase=srcbase, tile=tile, dstbase=dstbase: y.dma_start(
                    k_both[dstbase:dstbase + 64, kv * 2048:(kv + 1) * 2048],
                    k_nat[srcbase:srcbase + 64, tile * 2048:(tile + 1) * 2048]), dma=True))
        K.wait_only('t', hkb)
        maybe_dbg('knat', k_nat, [128, 4096])
        kls.close()
        phA = K.snap()
        maybe_dbg('kboth', k_both, [128, 8192])
        maybe_dbg('vaug', vaug, [128, 4160])

        wqo = K.sbuf("l1_wqo", [128, 8 * 1024], BF16, ls)
        cqt = K.sbuf("l1_cqt", [128, 512], F32, ls)
        sqt = K.sbuf("l1_sqt", [128, 512], F32, ls)
        qrt = K.sbuf("l1_qrt", [128, 1024], BF16, ls)
        q_fm = K.sbuf("l1_qfm", [128, 8 * 512], BF16, ls)
        ptt = K.sbuf("l1_ptt", [128, 2 * 512], BF16, ls)
        o_tmq = K.sbuf("l1_otm", [128, 4 * 1024], BF16, ls)
        o_fmq = K.sbuf("l1_ofm", [128, 8 * 512], BF16, ls)
        rvq = K.sbuf("l1_rv", [128, 4], F32, ls)

        h_wo_use = []
        for qc in range(4):
            _, hwq = load_wT(ls, "L1_Wq", D, D, deps=join + h_wo_use + (phA if qc == 0 else []), dst=wqo)
            h_wo_use = []
            if qc == 0:
                maybe_dbg('wqo', wqo, [128, 8192])
            hqf = []
            for ttl in range(4):
                tt = qc * 4 + ttl
                hct = K.op('y', [('t', K.n['t']), ('v', K.n['v'])], lambda y, tt=tt: y.dma_start(
                    cqt[:], W["L1_cosq"].ap().rearrange("(tt p) f -> p tt f", p=128)[:, tt, :]), dma=True)
                hst = K.op('y', [], lambda y, tt=tt: y.dma_start(
                    sqt[:], W["L1_sinq"].ap().rearrange("(tt p) f -> p tt f", p=128)[:, tt, :]), dma=True)
                for oc in range(2):
                    b = PB[4 + oc]
                    mms = []
                    for kt in range(8):
                        mms.append(lambda t, st, sp, kt=kt, tt=tt, oc=oc, b=b: t.matmul(
                            b['ap'][:], XNF[:, kt * 2048 + tt * 128: kt * 2048 + (tt + 1) * 128],
                            wqo[:, kt * 1024 + oc * 512: kt * 1024 + (oc + 1) * 512], start=st, stop=sp))
                    mm_chain(K, b, mms, [hwq, hct, hst])
                    if DBG == 'qpsum' and tt == 0 and oc == 0:
                        bank_read(K, 'v', b, [], lambda v, b=b: v.tensor_copy(rsc[:], b['ap'][:]))
                        maybe_dbg('qpsum', rsc, [128, 512])
                    pe = b['ap'].ap().rearrange("p (f two) -> p f two", two=2)
                    ce = cqt[:, oc * 256:(oc + 1) * 256]
                    se = sqt[:, oc * 256:(oc + 1) * 256]
                    t1 = rsc[:, 0:256]; t2 = rsc[:, 256:512]
                    qre = qrt.ap().rearrange("p (oc f two) -> p oc f two", two=2, oc=2)[:, oc]
                    bank_read(K, 'v', b, [('t', K.n['t'])], lambda v, pe=pe, ce=ce, t1=t1: v.tensor_tensor(t1, pe[:, :, 0], ce, ALU.mult))
                    K.op('v', [], lambda v, pe=pe, se=se, t2=t2: v.tensor_tensor(t2, pe[:, :, 1], se, ALU.mult))
                    K.op('v', [], lambda v, qre=qre, t1=t1, t2=t2: v.tensor_sub(qre[:, :, 0], t1, t2))
                    K.op('v', [], lambda v, pe=pe, se=se, t1=t1: v.tensor_tensor(t1, pe[:, :, 0], se, ALU.mult))
                    K.op('v', [], lambda v, pe=pe, ce=ce, t2=t2: v.tensor_tensor(t2, pe[:, :, 1], ce, ALU.mult))
                    hqr = K.op('v', [], lambda v, qre=qre, t1=t1, t2=t2: v.tensor_add(qre[:, :, 1], t1, t2))
                    b['ready'].append(hqr)
                for dt in range(8):
                    qb = QB[dt % 2]
                    hh = K.op('t', (qb['ready'] + [hqr]), lambda t, tt=tt, ttl=ttl, dt=dt, qb=qb: t.transpose(
                        qb['ap'][:, 0:128],
                        qrt[:, dt * 128:(dt + 1) * 128], IDENT[:]))
                    qb['wrote'] = hh
                    qb['ready'] = []
                    hqf.append(bank_read(K, 'v', qb, [], lambda v, ttl=ttl, dt=dt, qb=qb: v.tensor_copy(
                        q_fm[:, dt * 512 + ttl * 128: dt * 512 + (ttl + 1) * 128], qb['ap'][:, 0:128])))
            if qc == 0:
                maybe_dbg('qfm', q_fm, [128, 4096])
            for h_ in range(16):
                kv = h_ // 4
                bse = 64 * (h_ % 2)
                dtile = h_ // 2
                bpv = PB[3]
                pvh = [None, None]
                for kt in range(NTT):
                    b = PB[kt % 2]
                    mm_chain(K, b, [lambda t, st, sp, kt=kt, b=b, kv=kv, bse=bse, dtile=dtile: t.matmul(
                        b['ap'][:], k_both[bse:bse + 64, kv * 2048 + kt * 128: kv * 2048 + (kt + 1) * 128],
                        q_fm[bse:bse + 64, dtile * 512:(dtile + 1) * 512],
                        start=True, stop=True, tile_position=(bse, 0))],
                        (hqf[-1:] + hkb) if (kt == 0 and h_ == 0) else [])
                    hexp = bank_read(K, 's', b, [pvh[kt % 2]] if pvh[kt % 2] else [],
                                     lambda s, kt=kt, b=b: s.activation(
                        ptt[:, (kt % 2) * 512:(kt % 2 + 1) * 512], b['ap'][:], AF.Exp, scale=0.03125))
                    for qt in range(4):
                        hpv = K.op('t', [hexp] + (bpv['ready'] if kt == 0 and qt == 0 else []),
                                   lambda t, kt=kt, qt=qt, kv=kv, bpv=bpv: t.matmul(
                            bpv['ap'][:, qt * 65:(qt + 1) * 65],
                            ptt[:, (kt % 2) * 512 + qt * 128: (kt % 2) * 512 + qt * 128 + 128],
                            vaug[:, kt * 260 + kv * 65: kt * 260 + (kv + 1) * 65],
                            start=(kt == 0), stop=(kt == NTT - 1)))
                    pvh[kt % 2] = hpv
                    bpv['wrote'] = hpv
                    bpv['ready'] = []
                hrv = bank_read(K, 'v', bpv, [], lambda v, bpv=bpv: v.reciprocal(
                    rvq[:, 0:4], bpv['ap'][:, 0:260].rearrange("p (qt c) -> p qt c", c=65)[:, :, 64]))
                for qt in range(4):
                    bank_read(K, 'v', bpv, [hrv], lambda v, qt=qt, bpv=bpv, h_=h_: v.tensor_scalar_mul(
                        o_tmq[:, qt * 1024 + h_ * 64: qt * 1024 + h_ * 64 + 64],
                        bpv['ap'][:, qt * 65: qt * 65 + 64], rvq[:, qt:qt + 1]))
            if qc == 0:
                maybe_dbg('otmq', o_tmq, [128, 4096])
            _, hwo = load_wT(ls, "L1_Wo", D, D, deps=[('t', K.n['t']), ('s', K.n['s'])], dst=wqo)
            h_os = ('s', K.n['s'])
            hofm = []
            for dt in range(8):
                for ttl in range(4):
                    qb = QB[ttl % 2]
                    hh = K.op('t', (qb['ready'] + [h_os]), lambda t, dt=dt, ttl=ttl, qb=qb: t.transpose(
                        qb['ap'][:, 0:128],
                        o_tmq[:, ttl * 1024 + dt * 128: ttl * 1024 + (dt + 1) * 128], IDENT[:]))
                    qb['wrote'] = hh
                    qb['ready'] = []
                    hofm.append(bank_read(K, 'v', qb, [], lambda v, dt=dt, ttl=ttl, qb=qb: v.tensor_copy(
                        o_fmq[:, dt * 512 + ttl * 128: dt * 512 + (ttl + 1) * 128], qb['ap'][:, 0:128])))
            for ttl in range(4):
                for oc in range(2):
                    b = PB[4 + (ttl * 2 + oc) % 2]
                    mms = []
                    for kt in range(8):
                        mms.append(lambda t, st, sp, kt=kt, ttl=ttl, oc=oc, b=b: t.matmul(
                            b['ap'][:], o_fmq[:, kt * 512 + ttl * 128: kt * 512 + (ttl + 1) * 128],
                            wqo[:, kt * 1024 + oc * 512: kt * 1024 + (oc + 1) * 512], start=st, stop=sp))
                    mm_chain(K, b, mms, [hwo, hofm[-1]] if (ttl == 0 and oc == 0) else [])
                    hu = x_add(b, qc * 4 + ttl, oc)
                    h_wo_use.append(hu)
        ls.close()
        h_attn = K.snap()
        if L1_SKIP_FFN:
            return h_attn

        fls = contextlib.ExitStack()
        jn = join + [('t', K.n['t']), ('v', K.n['v']), ('s', K.n['s'])]
        ONESC = K.sbuf("l1_onesc", [128, 1], BF16, fls)
        h_onesc = K.op('v', jn, lambda v: v.memset(ONESC[:], 1.0))
        sinit = K.sbuf("l1_sinit", [64, 1024], F32, fls)
        sscale = K.sbuf("l1_sscale", [64, 1], F32, fls)
        h_si = K.op('y', jn, lambda y: y.dma_start(sinit[:], W["L1_slot_init"][:, :]), dma=True)
        h_ss = K.op('y', jn, lambda y: y.dma_start(sscale[:], W["L1_slot_scale"][:, :]), dma=True)
        hs_f32 = K.sbuf("l1_hsf", [64, 1024], F32, fls)
        hs_bf = K.sbuf("l1_hsb", [64, 1024], BF16, fls)
        h_hs = K.op('v', [h_si, h_ss], lambda v: v.tensor_scalar_mul(hs_f32[:], sinit[:], sscale[:, 0:1]))
        h_hsb = K.op('s', [h_hs], lambda s: s.copy(hs_bf[:], hs_f32[:]))
        s_fm = K.sbuf("l1_sfm", [128, 8 * 64], BF16, fls)
        for dc in range(8):
            qb = QB[dc % 2]
            hh = K.op('t', (qb['ready'] + [h_hsb]), lambda t, dc=dc, qb=qb: t.transpose(
                qb['ap'][:, 0:64], hs_bf[:, dc * 128:(dc + 1) * 128], IDENT[0:64, 0:64]))
            qb['wrote'] = hh
            qb['ready'] = []
            bank_read(K, 'v', qb, [], lambda v, dc=dc, qb=qb: v.tensor_copy(
                s_fm[:, dc * 64:(dc + 1) * 64], qb['ap'][:, 0:64]))
        smc = K.sbuf("l1_smc", [128, 8], BF16, fls)
        b_sm = PB[5]
        for dc in range(8):
            mm_chain(K, b_sm, [lambda t, st, sp, dc=dc: t.matmul(
                b_sm['ap'][:, dc:dc + 1], hs_bf[:, dc * 128:(dc + 1) * 128],
                ONESC[0:64, 0:1], start=True, stop=True)], [h_hsb, h_onesc])
        hsmc = bank_read(K, 's', b_sm, [], lambda s: s.activation(
            smc[:], b_sm['ap'][:, 0:8], AF.Identity, scale=1.0 / 64.0))
        b1t = K.sbuf("l1_b1t", [128, 4], F32, fls)
        invTc = K.sbuf("l1_invT", [128, 1], F32, fls)
        vst = K.sbuf("l1_vst", [128, 4], F32, fls)
        alpha_tm = K.sbuf("l1_alt", [128, NTT * 64], BF16, fls)
        alpha2_tm = K.sbuf("l1_al2t", [128, NTT * 64], BF16, fls)
        taut = K.sbuf("l1_taut", [1, 1], F32, fls)
        alpha_fm = K.sbuf("l1_alf", [64, 2048], BF16, fls)
        alpha2_fm = K.sbuf("l1_al2f", [64, 2048], BF16, fls)
        rw = K.sbuf("l1_rw", [64, 1], F32, fls)
        gi_fm = K.sbuf("l1_gif", [128, 8 * 64], BF16, fls)
        hn_fm = K.sbuf("l1_hnfm", [128, 8 * 64], BF16, fls)
        sv_sb = K.sbuf("l1_svsb", [64, 1024], BF16, fls)
        hls = contextlib.ExitStack()
        w1a, hw1a = load_wT(hls, "L1_w1aT", D, 512, deps=jn)
        w1b, hw1b = load_wT(hls, "L1_w1bT", D, 512, deps=jn)
        b1c, h_b1 = load_cols(hls, "L1_b1", 512, jn)
        bb1 = PB[5]
        for mc in range(4):
            mms = []
            for kt in range(8):
                mms.append(lambda t, st, sp, kt=kt, mc=mc: t.matmul(
                    bb1['ap'][:, mc:mc + 1], w1b[:, kt * 512 + mc * 128: kt * 512 + (mc + 1) * 128],
                    smc[:, kt:kt + 1], start=st, stop=sp))
            mm_chain(K, bb1, mms, [hw1b, hsmc])
        h_b1t = bank_read(K, 'v', bb1, [h_b1], lambda v: v.tensor_add(b1t[:], bb1['ap'][:, 0:4], b1c[:]))
        h_fm = K.sbuf("l1_hfm", [128, 4 * 2048], BF16, hls)
        hh_done = []
        for hc in range(4):
            for tc in range(4):
                b = PB[(hc * 4 + tc) % 2]
                mms = []
                for kt in range(8):
                    mms.append(lambda t, st, sp, kt=kt, hc=hc, tc=tc, b=b: t.matmul(
                        b['ap'][:], w1a[:, kt * 512 + hc * 128: kt * 512 + (hc + 1) * 128],
                        XNF[:, kt * 2048 + tc * 512: kt * 2048 + (tc + 1) * 512], start=st, stop=sp))
                mm_chain(K, b, mms, [hw1a, h_b1t] if (hc == 0 and tc == 0) else [])
                hh_done.append(bank_read(K, 's', b, [], lambda s, hc=hc, tc=tc, b=b: s.activation(
                    h_fm[:, hc * 2048 + tc * 512: hc * 2048 + (tc + 1) * 512], b['ap'][:],
                    AF.Gelu, bias=b1t[:, hc:hc + 1])))
        maybe_dbg('hfm', h_fm, [128, 8192])
        w2t, hw2 = load_wT(hls, "L1_w2T", 512, 64, deps=jn)
        b2r, h_b2 = load_row(hls, "L1_b2", 64, jn, dt=BF16, cast=True)
        h_tau0 = K.op('y', jn, lambda y: y.dma_start(taut[:], W["L1_tau"][:, :]), dma=True)
        h_ta = K.op('s', [h_tau0], lambda s: s.activation(taut[:], taut[:], AF.Abs))
        h_tb = K.op('v', [h_ta], lambda v: v.tensor_scalar_add(taut[:], taut[:], 0.1))
        h_tc = K.op('v', [h_tb], lambda v: v.reciprocal(taut[:], taut[:]))
        h_invT = bcast_row(invTc, taut, 1, [h_tc], PB[5])
        lwf = K.sbuf("l1_lw", [128, NTT * 64], F32, hls)
        lwk = K.sbuf("l1_lwk", [128, 64], F32, hls)
        negbig = K.sbuf("l1_negbig", [128, 64], F32, hls)
        h_nb = K.op('v', jn, lambda v: v.memset(negbig[:], -1e30))
        msk = K.sbuf("l1_msk", [128, 64], F32, hls)
        msku = K.sbuf("l1_msku", [128, 64], mybir.dt.uint8, hls)
        ha2 = []
        for tt in range(NTT):
            b = PB[2 + tt % 2]
            mms = []
            for kt in range(4):
                mms.append(lambda t, st, sp, kt=kt, tt=tt, b=b: t.matmul(
                    b['ap'][:, 0:64], h_fm[:, kt * 2048 + tt * 128: kt * 2048 + (tt + 1) * 128],
                    w2t[:, kt * 64:(kt + 1) * 64], start=st, stop=False))
            mms.append(lambda t, st, sp, b=b: t.matmul(
                b['ap'][:, 0:64], ONESR[:, 0:128], b2r[:, 0:64], start=False, stop=True))
            mm_chain(K, b, mms, [hw2, h_b2, hh_done[-1]] if tt == 0 else [])
            hl = bank_read(K, 'v', b, [h_nb, h_invT], lambda v, tt=tt, b=b: v.tensor_copy(
                lwf[:, tt * 64:(tt + 1) * 64], b['ap'][:, 0:64]))
            hl2 = K.op('v', ha2[-1:], lambda v, tt=tt: v.tensor_copy(lwk[:], lwf[:, tt * 64:(tt + 1) * 64]))
            hv1c = K.op('v', [], lambda v: v.reduce_max(vst[:, 0:1], lwk[:], axis=AX.X))
            hlast = hv1c
            for it in range(8):
                hm = K.op('v', [], lambda v: v.tensor_scalar(msku[:], lwk[:], vst[:, 0:1] if it == 0 else vst[:, 2:3], None, ALU.is_ge))
                hsel = K.op('v', [], lambda v: v.select(lwk[:], msku[:], negbig[:], lwk[:]))
                if it < 7:
                    hlast = K.op('v', [], lambda v: v.reduce_max(vst[:, 2:3], lwk[:], axis=AX.X))
            hmt = K.op('v', [], lambda v: v.tensor_scalar(msk[:], lwk[:], CVC[:, 3:4], None, ALU.is_le))
            heb = K.op('v', [], lambda v: v.scalar_tensor_tensor(vst[:, 1:2], vst[:, 0:1], -1.0, invTc[:], ALU.mult, ALU.mult))
            hex_ = K.op('s', [heb, hl, hmt], lambda s, tt=tt: s.activation(
                lwk[:], lwf[:, tt * 64:(tt + 1) * 64], AF.Exp, bias=vst[:, 1:2], scale=invTc[:]))
            hem = K.op('v', [hex_, hmt], lambda v: v.tensor_mul(lwk[:], lwk[:], msk[:]))
            hz = K.op('v', [], lambda v: v.reduce_sum(vst[:, 2:3], lwk[:], axis=AX.X))
            hrz = K.op('v', [], lambda v: v.reciprocal(vst[:, 3:4], vst[:, 2:3]))
            ha2.append(K.op('s', [hrz], lambda s, tt=tt: s.activation(
                alpha_tm[:, tt * 64:(tt + 1) * 64], lwk[:], AF.Identity, scale=vst[:, 3:4])))
        hls.close()
        jn2 = [('t', K.n['t']), ('v', K.n['v']), ('s', K.n['s'])]
        maybe_dbg('alpha', alpha_tm, [128, 1024])
        bws = PB[4]
        mms = []
        for tt in range(NTT):
            mms.append(lambda t, st, sp, tt=tt: t.matmul(
                bws['ap'][0:64, 0:1], alpha_tm[:, tt * 64:(tt + 1) * 64],
                ONESC[:, 0:1], start=st, stop=sp))
        mm_chain(K, bws, mms, [ha2[-1], h_onesc])
        hrw0 = bank_read(K, 'v', bws, [h_cv], lambda v: v.tensor_scalar_add(rw[:], bws['ap'][0:64, 0:1], C_WSEPS[0:64, :]))
        hrw = K.op('v', [hrw0], lambda v: v.reciprocal(rw[:], rw[:]))
        ha2t = []
        for tt in range(NTT):
            qb = QB[tt % 2]
            hh = K.op('t', (qb['ready'] + [ha2[-1]]), lambda t, tt=tt, qb=qb: t.transpose(
                qb['ap'][0:64, 0:128], alpha_tm[:, tt * 64:(tt + 1) * 64], IDENT[:]))
            qb['wrote'] = hh
            qb['ready'] = []
            bank_read(K, 'v', qb, [], lambda v, tt=tt, qb=qb: v.tensor_copy(
                alpha_fm[:, tt * 128:(tt + 1) * 128], qb['ap'][0:64, 0:128]))
        hsc = K.op('s', [hrw, ('v', K.n['v'])], lambda s: s.activation(
            alpha2_fm[:], alpha_fm[:], AF.Identity, scale=rw[:, 0:1]))
        for tt in range(NTT):
            qb = QB[tt % 2]
            hh = K.op('t', (qb['ready'] + [hsc]), lambda t, tt=tt, qb=qb: t.transpose(
                qb['ap'][:, 0:64], alpha2_fm[:, tt * 128:(tt + 1) * 128], IDENT[0:64, 0:64]))
            qb['wrote'] = hh
            qb['ready'] = []
            ha2t.append(bank_read(K, 'v', qb, [], lambda v, tt=tt, qb=qb: v.tensor_copy(
                alpha2_tm[:, tt * 64:(tt + 1) * 64], qb['ap'][:, 0:64])))
        for dc in range(8):
            b = PB[dc % 2]
            mms = []
            for tt in range(NTT):
                mms.append(lambda t, st, sp, tt=tt, dc=dc, b=b: t.matmul(
                    b['ap'][:, 0:64], XNT[:, tt * 1024 + dc * 128: tt * 1024 + (dc + 1) * 128],
                    alpha2_tm[:, tt * 64:(tt + 1) * 64], start=st, stop=sp))
            mm_chain(K, b, mms, [ha2t[-1]] if dc == 0 else [])
            bank_read(K, 's', b, [], lambda s, dc=dc, b=b: s.copy(
                gi_fm[:, dc * 64:(dc + 1) * 64], b['ap'][:, 0:64]))
        gls = contextlib.ExitStack()
        jn3 = jn2 + [('dy', K.n['dy'])]
        gxT = K.sbuf("l1_gxT", [128, 24 * 64], F32, gls)
        ghT = K.sbuf("l1_ghT", [128, 24 * 64], F32, gls)
        bihc, h_bih = load_cols(gls, "L1_bih", 3072, jn3)
        bhhc, h_bhh = load_cols(gls, "L1_bhh", 3072, jn3)
        wgt = K.sbuf("l1_wgt", [128, 3072], BF16, gls)
        for which, (wname, src_fm, dst, bcol, hb) in enumerate([
                ("L1_wihT", gi_fm, gxT, bihc, h_bih), ("L1_whhT", s_fm, ghT, bhhc, h_bhh)]):
            for grp in range(4):
                banks = [PB[j] for j in range(6)]
                for kt in range(8):
                    _, hwg = load_wT(gls, wname, D, 3072, deps=[('t', K.n['t'])], dst=wgt, kt0=kt, nkt=1)
                    for j in range(6):
                        ch = grp * 6 + j
                        b = banks[j]
                        d = [hwg] + (b['ready'] if kt == 0 else [])
                        hmm = K.op('t', d, lambda t, kt=kt, ch=ch, b=b, src_fm=src_fm: t.matmul(
                            b['ap'][:, 0:64], wgt[:, ch * 128:(ch + 1) * 128],
                            src_fm[:, kt * 64:(kt + 1) * 64], start=(kt == 0), stop=(kt == 7)))
                        b['wrote'] = hmm
                        if kt == 0:
                            b['ready'] = []
                for j in range(6):
                    ch = grp * 6 + j
                    bank_read(K, 's', banks[j], [], lambda s, ch=ch, b=banks[j], dst=dst, bcol=bcol: s.activation(
                        dst[:, ch * 64:(ch + 1) * 64], b['ap'][:, 0:64],
                        AF.Identity, bias=bcol[:, ch:ch + 1]))
        maybe_dbg('gxT', gxT, [128, 1536])
        maybe_dbg('ghT', ghT, [128, 1536])
        hsTf = K.sbuf("l1_hsTf", [128, 8 * 64], F32, gls)
        for dc in range(8):
            qb = QB[dc % 2]
            hh = K.op('t', (qb['ready'] + [h_hsb]), lambda t, dc=dc, qb=qb: t.transpose(
                qb['ap'][:, 0:64], hs_bf[:, dc * 128:(dc + 1) * 128], IDENT[0:64, 0:64]))
            qb['wrote'] = hh
            qb['ready'] = []
            bank_read(K, 'v', qb, [], lambda v, dc=dc, qb=qb: v.tensor_copy(
                hsTf[:, dc * 64:(dc + 1) * 64], qb['ap'][:, 0:64]))
        g1 = K.sbuf("l1_g1", [128, 512], F32, gls)
        g2 = K.sbuf("l1_g2", [128, 512], F32, gls)
        hnT = K.sbuf("l1_hnT", [128, 512], F32, gls)
        jv = [('v', K.n['v']), ('s', K.n['s'])]
        h1 = K.op('v', jv, lambda v: v.tensor_add(g1[:], gxT[:, 0:512], ghT[:, 0:512]))
        hr_ = K.op('s', [h1], lambda s: s.activation(g1[:], g1[:], AF.Sigmoid))
        hn2 = K.op('v', [hr_], lambda v: v.tensor_mul(g2[:], g1[:], ghT[:, 1024:1536]))
        hn3 = K.op('v', [hn2], lambda v: v.tensor_add(g2[:], g2[:], gxT[:, 1024:1536]))
        hn4 = K.op('s', [hn3], lambda s: s.activation(hnT[:], g2[:], AF.Tanh))
        hz1 = K.op('v', jv, lambda v: v.tensor_add(g1[:], gxT[:, 512:1024], ghT[:, 512:1024]))
        hz2 = K.op('s', [hz1, hn4], lambda s: s.activation(g1[:], g1[:], AF.Sigmoid))
        hh1 = K.op('v', [hz2, hn4], lambda v: v.tensor_mul(g2[:], g1[:], hnT[:]))
        hh2 = K.op('v', [hh1], lambda v: v.tensor_sub(hnT[:], hnT[:], g2[:]))
        hh3 = K.op('v', [('v', K.n['v'])], lambda v: v.tensor_mul(g2[:], g1[:], hsTf[:]))
        hh4 = K.op('v', [hh3], lambda v: v.tensor_add(hnT[:], hnT[:], g2[:]))
        hh5 = K.op('s', [hh4], lambda s: s.copy(hn_fm[:], hnT[:]))
        hnf = [hh5]
        gls.close()
        jn4 = [('t', K.n['t']), ('v', K.n['v']), ('s', K.n['s']), ('dy', K.n['dy'])]
        wvt, hwvt = load_wT(fls, "L1_wvT", D, 1024, deps=jn4)
        bvr, h_bv = load_row(fls, "L1_bv", D, jn4, dt=BF16, cast=True)
        for oc in range(2):
            b = PB[4 + oc]
            mms = []
            for kt in range(8):
                mms.append(lambda t, st, sp, kt=kt, oc=oc, b=b: t.matmul(
                    b['ap'][0:64, :], hn_fm[:, kt * 64:(kt + 1) * 64],
                    wvt[:, kt * 1024 + oc * 512: kt * 1024 + (oc + 1) * 512], start=st, stop=False))
            mms.append(lambda t, st, sp, oc=oc, b=b: t.matmul(
                b['ap'][0:64, :], ONESR[:, 0:64], bvr[:, oc * 512:(oc + 1) * 512], start=False, stop=True))
            mm_chain(K, b, mms, [hwvt, h_bv, hnf[-1]] if oc == 0 else [])
            bank_read(K, 's', b, [], lambda s, oc=oc, b=b: s.copy(
                sv_sb[:, oc * 512:(oc + 1) * 512], b['ap'][0:64, :]))
        maybe_dbg('svsb', sv_sb, [64, 1024])
        maybe_dbg('a2tm', alpha2_tm, [128, 1024])
        of_fm = K.sbuf("l1_offm", [128, 8 * 512], BF16, fls)
        wot, hwot = load_wT(fls, "L1_woT", D, D, deps=jn4)
        bor, h_bo = load_row(fls, "L1_bo", D, jn4, dt=BF16, cast=True)
        for tc in range(4):
            hof = []
            for dc in range(8):
                b = PB[dc % 2]
                mm_chain(K, b, [lambda t, st, sp, dc=dc, tc=tc, b=b: t.matmul(
                    b['ap'][:], sv_sb[:, dc * 128:(dc + 1) * 128],
                    alpha_fm[:, tc * 512:(tc + 1) * 512], start=True, stop=True)],
                    [('s', K.n['s']), ('v', K.n['v']), ('t', K.n['t'])] if (dc == 0 and tc == 0) else [])
                hof.append(bank_read(K, 's', b, [], lambda s, dc=dc, b=b: s.copy(
                    of_fm[:, dc * 512:(dc + 1) * 512], b['ap'][:])))
            for ttl in range(4):
                tt = tc * 4 + ttl
                for oc in range(2):
                    b = PB[2 + (ttl * 2 + oc) % 2]
                    mms = []
                    for kt in range(8):
                        mms.append(lambda t, st, sp, kt=kt, ttl=ttl, oc=oc, b=b: t.matmul(
                            b['ap'][:], of_fm[:, kt * 512 + ttl * 128: kt * 512 + (ttl + 1) * 128],
                            wot[:, kt * 1024 + oc * 512: kt * 1024 + (oc + 1) * 512], start=st, stop=False))
                    mms.append(lambda t, st, sp, oc=oc, b=b: t.matmul(
                        b['ap'][:], ONESR[:, 0:128], bor[:, oc * 512:(oc + 1) * 512], start=False, stop=True))
                    mm_chain(K, b, mms, [hwot, h_bo, hof[-1]] if (ttl == 0 and oc == 0) else [])
                    x_add(b, tt, oc)
        fls.close()
        ls.close()
        return K.snap()


    def emit_l2(join):
        ls = contextlib.ExitStack()
        wc, h_wc = None, None
        conv_c = K.sbuf("l2_convc", [128, 64], F32, ls)
        h_cw = K.op('y', join, lambda y: y.dma_start(
            conv_c.ap().rearrange("p (ct k) -> p ct k", k=4),
            W["L2_conv_w"].ap().rearrange("(ct p) k -> p ct k", p=128)), dma=True)
        bdt, h_bdt = load_cols(ls, "L2_b_dt", 2048, join)
        dpc, h_dp = load_cols(ls, "L2_D", 2048, join)
        wx, h_wx = load_wT(ls, "L2_W_x", 2048, 32, deps=join)
        wdt = K.sbuf("l2_wdt", [16, 2048], BF16, ls)
        h_wdt = K.op('g', join, lambda g: g.dma_start(wdt[:], W["L2_W_dt"][:, :]), dma=True)
        wbp = K.sbuf("l2_wbp", [16, 2048], BF16, ls)
        h_wbp = K.op('g', join, lambda g: g.dma_start(wbp[:], W["L2_W_Bp"][:, :]), dma=True)
        state = K.sbuf("l2_state", [128, 16], F32, ls)
        h_st0 = K.op('v', join, lambda v: v.memset(state[:], 0.0))
        xtail = K.sbuf("l2_xtail", [128, 16 * 3], F32, ls)
        h_xt0 = K.op('v', join, lambda v: v.memset(xtail[:], 0.0))

        u = K.sbuf("l2_u", [128, 16 * 1024], BF16, ls)
        wout = K.sbuf("l2_wout", [128, 8 * 1024], BF16, ls)
        xc_sb = K.sbuf("l2_xc", [128, 1027], F32, ls)
        S1 = K.sbuf("l2_s1", [128, 1024], F32, ls)
        S2 = K.sbuf("l2_s2", [128, 1024], F32, ls)
        xp1 = K.sbuf("l2_xp1", [16, 1024], BF16, ls)
        xp2 = K.sbuf("l2_xp2", [16, 1024], BF16, ls)
        win = K.sbuf("l2_win", [128, 2 * 8 * 128], BF16, ls)
        h_prev_u = [('v', 1 if False else K.n['v'])]

        hu_done = []
        h_win_reads = [[], []]
        h_scan = [h_st0, h_xt0]
        for th in range(2):
            for ct in range(16):
                slot = ct % 2
                hwin = K.op('g', (join if (th == 0 and ct < 2) else []) + h_win_reads[slot],
                            lambda g, ct=ct, slot=slot: g.dma_start(
                    win[:, slot * 1024:(slot + 1) * 1024].rearrange("p (kt n) -> p kt n", n=128),
                    W["L2_W_in"].ap().rearrange("(kt p) n -> p kt n", p=128)[:, :, ct * 128:(ct + 1) * 128]),
                    dma=True)
                h_win_reads[slot] = []
                hxc = []
                for tc2 in range(2):
                    b = PB[tc2]
                    mms = []
                    for kt in range(8):
                        mms.append(lambda t, st, sp, kt=kt, ct=ct, th=th, tc2=tc2, b=b, slot=slot: t.matmul(
                            b['ap'][:], win[:, slot * 1024 + kt * 128:(slot * 1024) + (kt + 1) * 128],
                            XNF[:, kt * 2048 + th * 1024 + tc2 * 512: kt * 2048 + th * 1024 + (tc2 + 1) * 512],
                            start=st, stop=sp))
                    hmm = mm_chain(K, b, mms, [hwin])
                    h_win_reads[slot].append(hmm)
                    hxc.append(bank_read(K, 'v', b, ([h_xt0] if (th == 0 and ct == 0 and tc2 == 0) else []) + ([('s', K.n['s'])] if (ct == 0 and tc2 == 0) else []),
                                         lambda v, tc2=tc2, b=b: v.tensor_copy(
                        xc_sb[:, 3 + tc2 * 512: 3 + (tc2 + 1) * 512], b['ap'][:])))
                hxt = K.op('v', [hxc[-1]] + h_scan, lambda v, ct=ct: v.tensor_copy(
                    xc_sb[:, 0:3], xtail[:, ct * 3:(ct + 1) * 3]))
                hxt2 = K.op('v', [hxt], lambda v, ct=ct: v.tensor_copy(
                    xtail[:, ct * 3:(ct + 1) * 3], xc_sb[:, 1024:1027]))
                hcv = K.op('v', [hxt2, h_cw] + hu_done[-1:], lambda v, ct=ct: v.tensor_scalar(
                    S1[:], xc_sb[:, 0:1024], conv_c[:, ct * 4:ct * 4 + 1], None, ALU.mult))
                for k_ in range(1, 4):
                    hcv = K.op('v', [], lambda v, ct=ct, k_=k_: v.scalar_tensor_tensor(
                        S1[:], xc_sb[:, k_:k_ + 1024], conv_c[:, ct * 4 + k_:ct * 4 + k_ + 1],
                        S1[:], ALU.mult, ALU.add))
                hu = K.op('s', [hcv], lambda s, ct=ct: s.activation(
                    u[:, ct * 1024:(ct + 1) * 1024], S1[:], AF.Silu))
                hu_done.append(hu)
            hxp_ = {0: [], 1: []}
            for j2 in range(2):
                b = PB[4 + j2]
                for tc2 in range(2):
                    mms = []
                    for kt in range(16):
                        mms.append(lambda t, st, sp, kt=kt, j2=j2, tc2=tc2, b=b: t.matmul(
                            b['ap'][0:16, 0:512],
                            wx[:, kt * 32 + j2 * 16: kt * 32 + (j2 + 1) * 16],
                            u[:, kt * 1024 + tc2 * 512: kt * 1024 + (tc2 + 1) * 512], start=st, stop=sp))
                    mm_chain(K, b, mms, [hu_done[-1], h_wx] if (j2 == 0 and tc2 == 0) else [])
                    hxp_[j2].append(bank_read(K, 'v', b, [], lambda v, j2=j2, tc2=tc2, b=b: v.tensor_copy(
                        (xp1 if j2 == 0 else xp2)[0:16, tc2 * 512:(tc2 + 1) * 512], b['ap'][0:16, 0:512])))
            hxp = [hxp_[0][-1], hxp_[1][-1]]
            h_scan = []
            for ct in range(16):
                bdt_ = PB[ct % 2]
                bbp_ = PB[2 + ct % 2]
                bz_ = PB[4 + ct % 2]
                for tc2 in range(2):
                    mm_chain(K, bdt_, [lambda t, st, sp, ct=ct, tc2=tc2, b=bdt_: t.matmul(
                        b['ap'][:], wdt[0:16, ct * 128:(ct + 1) * 128],
                        xp1[0:16, tc2 * 512:(tc2 + 1) * 512], start=True, stop=True)],
                        ([hxp[0], h_wdt] if (ct == 0 and tc2 == 0) else []))
                    bank_read(K, 's', bdt_, ([h_bdt] if (ct == 0 and tc2 == 0) else []) + (h_scan[-1:] if tc2 == 0 else []),
                              lambda s, ct=ct, tc2=tc2, b=bdt_: s.activation(
                        S1[:, tc2 * 512:(tc2 + 1) * 512], b['ap'][:],
                        AF.Exp, bias=bdt[:, ct:ct + 1]))
                    hsp1 = K.op('v', [('s', K.n['s'])], lambda v, tc2=tc2: v.tensor_scalar_add(
                        S1[:, tc2 * 512:(tc2 + 1) * 512], S1[:, tc2 * 512:(tc2 + 1) * 512], 1.0))
                    K.op('s', [hsp1], lambda s, tc2=tc2: s.activation(
                        S1[:, tc2 * 512:(tc2 + 1) * 512], S1[:, tc2 * 512:(tc2 + 1) * 512], AF.Ln))
                    mm_chain(K, bbp_, [lambda t, st, sp, ct=ct, tc2=tc2, b=bbp_: t.matmul(
                        b['ap'][:], wbp[0:16, ct * 128:(ct + 1) * 128],
                        xp2[0:16, tc2 * 512:(tc2 + 1) * 512], start=True, stop=True)],
                        ([hxp[1], h_wbp] if (ct == 0 and tc2 == 0) else []))
                    bank_read(K, 'v', bbp_, [hu_done[-1]], lambda v, ct=ct, tc2=tc2, b=bbp_: v.tensor_tensor(
                        S2[:, tc2 * 512:(tc2 + 1) * 512], u[:, ct * 1024 + tc2 * 512: ct * 1024 + (tc2 + 1) * 512],
                        b['ap'][:], ALU.mult))
                    hwin2 = K.op('g', h_win_reads[ct % 2], lambda g, ct=ct, th=th, slot=ct % 2: g.dma_start(
                        win[:, slot * 1024:(slot + 1) * 1024].rearrange("p (kt n) -> p kt n", n=128),
                        W["L2_W_in"].ap().rearrange("(kt p) n -> p kt n", p=128)[:, :, 2048 + ct * 128: 2048 + (ct + 1) * 128]),
                        dma=True) if tc2 == 0 else hwin2
                    h_win_reads[ct % 2] = [] if tc2 == 0 else h_win_reads[ct % 2]
                    mms = []
                    for kt in range(8):
                        mms.append(lambda t, st, sp, kt=kt, ct=ct, th=th, tc2=tc2, b=bz_: t.matmul(
                            b['ap'][:], win[:, (ct % 2) * 1024 + kt * 128: (ct % 2) * 1024 + (kt + 1) * 128],
                            XNF[:, kt * 2048 + th * 1024 + tc2 * 512: kt * 2048 + th * 1024 + (tc2 + 1) * 512],
                            start=st, stop=sp))
                    hz = mm_chain(K, bz_, mms, [hwin2])
                    h_win_reads[ct % 2].append(hz)
                    bank_read(K, 's', bz_, (h_scan[-1:] if tc2 == 0 else []), lambda s, tc2=tc2, b=bz_: s.activation(
                        xc_sb[:, tc2 * 512:(tc2 + 1) * 512], b['ap'][:], AF.Silu))
                hcl = K.op('v', [('s', K.n['s']), h_cv], lambda v: v.tensor_scalar(
                    S1[:], S1[:], C_CLIPLO, 1.0, ALU.max, ALU.min))
                ha_ = K.op('v', [hcl], lambda v: v.tensor_scalar(
                    S1[:], S1[:], -1.0, 1.0, ALU.mult, ALU.add))
                hsc = K.op('v', [ha_], lambda v, ct=ct: v.tensor_tensor_scan(
                    S2[:], S1[:], S2[:], state[:, ct:ct + 1], ALU.mult, ALU.add))
                hst = K.op('v', [hsc], lambda v, ct=ct: v.tensor_copy(
                    state[:, ct:ct + 1], S2[:, 1023:1024]))
                h_scan.append(hst)
                hy1 = K.op('v', [hst, h_dp], lambda v, ct=ct: v.scalar_tensor_tensor(
                    S2[:], u[:, ct * 1024:(ct + 1) * 1024], dpc[:, ct:ct + 1], S2[:], ALU.mult, ALU.add))
                hy2 = K.op('v', [hy1, ('s', K.n['s'])], lambda v, ct=ct: v.tensor_tensor(
                    u[:, ct * 1024:(ct + 1) * 1024], S2[:], xc_sb[:, 0:1024], ALU.mult))
                h_scan.append(hy2)
            for kh in range(2):
                _, hwo = load_wT(ls, "L2_W_out", 2048, D, deps=[('t', K.n['t'])], dst=wout,
                                 kt0=kh * 8, nkt=8)
                for ttl in range(8):
                    tt = th * 8 + ttl
                    for oc in range(2):
                        b = PB[(ttl * 2 + oc) % 4]
                        mms = []
                        for kt in range(8):
                            mms.append(lambda t, st, sp, kt=kt, kh=kh, ttl=ttl, oc=oc, b=b: t.matmul(
                                b['ap'][:], u[:, (kh * 8 + kt) * 1024 + ttl * 128: (kh * 8 + kt) * 1024 + (ttl + 1) * 128],
                                wout[:, kt * 1024 + oc * 512: kt * 1024 + (oc + 1) * 512], start=st, stop=sp))
                        mm_chain(K, b, mms, [hwo, h_scan[-1]] if (ttl == 0 and oc == 0) else [])
                        x_add(b, tt, oc)
        ls.close()

        fls = contextlib.ExitStack()
        jn = join + [('t', K.n['t']), ('v', K.n['v']), ('s', K.n['s']), ('dy', K.n['dy']), ('dg', K.n['dg'])]
        wg, h_wg = load_wT(fls, "L2_Wg", D, 8, deps=jn)
        lg = K.sbuf("l2_lg", [128, 64], F32, fls)
        wtm = K.sbuf("l2_wtm", [128, 16 * 8], F32, fls)
        msku = K.sbuf("l2_msku", [128, 8], mybir.dt.uint8, fls)
        m1f = K.sbuf("l2_m1f", [128, 8], F32, fls)
        nbg = K.sbuf("l2_nbg", [128, 8], F32, fls)
        h_nb = K.op('v', jn, lambda v: v.memset(nbg[:], -1e30))
        vst = K.sbuf("l2_vst", [128, 6], F32, fls)
        hwt = []
        for tt in range(NTT):
            b = PB[tt % 2]
            mms = []
            for kt in range(8):
                mms.append(lambda t, st, sp, kt=kt, tt=tt, b=b: t.matmul(
                    b['ap'][:, 0:8], XNF[:, kt * 2048 + tt * 128: kt * 2048 + (tt + 1) * 128],
                    wg[:, kt * 8:(kt + 1) * 8], start=st, stop=sp))
            mm_chain(K, b, mms, [h_wg] if tt == 0 else [])
            hl = bank_read(K, 'v', b, hwt[-1:], lambda v, b=b: v.tensor_copy(lg[:, 0:8], b['ap'][:, 0:8]))
            h1 = K.op('v', [hl, h_nb], lambda v: v.reduce_max(vst[:, 0:1], lg[:, 0:8], axis=AX.X))
            h2 = K.op('v', [h1], lambda v: v.tensor_scalar(m1f[:], lg[:, 0:8], vst[:, 0:1], None, ALU.is_ge))
            h2u = K.op('v', [h2], lambda v: v.tensor_scalar(msku[:], lg[:, 0:8], vst[:, 0:1], None, ALU.is_ge))
            h3 = K.op('v', [h2u], lambda v: v.select(lg[:, 8:16], msku[:], nbg[:], lg[:, 0:8]))
            h4 = K.op('v', [h3], lambda v: v.reduce_max(vst[:, 1:2], lg[:, 8:16], axis=AX.X))
            h5 = K.op('v', [h4], lambda v: v.tensor_scalar(vst[:, 2:3], vst[:, 1:2], vst[:, 0:1], None, ALU.subtract))
            h6 = K.op('s', [h5], lambda s: s.activation(vst[:, 3:4], vst[:, 2:3], AF.Exp))
            h7 = K.op('v', [h6], lambda v: v.tensor_scalar_add(vst[:, 4:5], vst[:, 3:4], 1.0))
            h8 = K.op('v', [h7], lambda v: v.reciprocal(vst[:, 4:5], vst[:, 4:5]))
            h9 = K.op('v', [h8], lambda v: v.tensor_tensor(vst[:, 5:6], vst[:, 3:4], vst[:, 4:5], ALU.mult))
            h10 = K.op('v', [h9], lambda v: v.tensor_scalar(lg[:, 8:16], lg[:, 8:16], vst[:, 1:2], None, ALU.is_ge))
            h11 = K.op('v', [h10], lambda v: v.tensor_scalar(lg[:, 8:16], lg[:, 8:16], vst[:, 5:6], None, ALU.mult))
            h12 = K.op('v', [h11], lambda v, tt=tt: v.scalar_tensor_tensor(
                wtm[:, tt * 8:(tt + 1) * 8], m1f[:], vst[:, 4:5], lg[:, 8:16], ALU.mult, ALU.add))
            hwt.append(h12)

        hs_fm = K.sbuf("l2_hsfm", [128, 11 * 512], BF16, fls)
        sA = K.sbuf("l2_sA", [128, 512], BF16, fls)
        wr = K.sbuf("l2_wr", [128, 2 * 8 * 128], BF16, fls)
        w3b = K.sbuf("l2_w3b", [128, 11 * 1024], BF16, fls)
        h_wr_reads = [[], []]
        h_w3_reads = []

        def ffn_pass(w1name, w2name, w3name, nhc, e=None, wcol=None, w3kt0=0, hdeps=()):
            nonlocal h_wr_reads, h_w3_reads
            h3 = K.op('g', list(hdeps) + h_w3_reads, lambda g: g.dma_start(
                w3b.ap().rearrange("p (kt n) -> p kt n", n=1024)[:, 0:nhc, :],
                (W[w3name].ap() if e is None else W[w3name].ap()[e])
                .rearrange("(kt p) n -> p kt n", p=128)[:, w3kt0:w3kt0 + nhc, :]), dma=True)
            h_w3_reads = []
            for tb in range(4):
                hhe = []
                for hc in range(nhc):
                    slot = hc % 2
                    c0 = (w3kt0 + hc) * 128
                    hw1 = K.op('g', h_wr_reads[slot] + (list(hdeps) if hc < 2 and tb == 0 else []),
                               lambda g, c0=c0, slot=slot: g.dma_start(
                        wr[:, slot * 1024: slot * 1024 + 1024].rearrange("p (kt n) -> p kt n", n=128),
                        (W[w1name].ap() if e is None else W[w1name].ap()[e])
                        .rearrange("(kt p) n -> p kt n", p=128)[:, :, c0:c0 + 128]), dma=True)
                    h_wr_reads[slot] = []
                    bA = PB[slot]
                    mms = []
                    for kt in range(8):
                        mms.append(lambda t, st, sp, kt=kt, tb=tb, slot=slot, b=bA: t.matmul(
                            b['ap'][:], wr[:, slot * 1024 + kt * 128: slot * 1024 + (kt + 1) * 128],
                            XNF[:, kt * 2048 + tb * 512: kt * 2048 + (tb + 1) * 512], start=st, stop=sp))
                    hmmA = mm_chain(K, bA, mms, [hw1])
                    h_wr_reads[slot].append(hmmA)
                    hsa = bank_read(K, 's', bA, [], lambda s, b=bA: s.activation(sA[:], b['ap'][:], AF.Silu))
                    hw2 = K.op('g', h_wr_reads[slot], lambda g, c0=c0, slot=slot: g.dma_start(
                        wr[:, slot * 1024: slot * 1024 + 1024].rearrange("p (kt n) -> p kt n", n=128),
                        (W[w2name].ap() if e is None else W[w2name].ap()[e])
                        .rearrange("(kt p) n -> p kt n", p=128)[:, :, c0:c0 + 128]), dma=True)
                    h_wr_reads[slot] = []
                    bB = PB[2 + slot]
                    mms = []
                    for kt in range(8):
                        mms.append(lambda t, st, sp, kt=kt, tb=tb, slot=slot, b=bB: t.matmul(
                            b['ap'][:], wr[:, slot * 1024 + kt * 128: slot * 1024 + (kt + 1) * 128],
                            XNF[:, kt * 2048 + tb * 512: kt * 2048 + (tb + 1) * 512], start=st, stop=sp))
                    hmmB = mm_chain(K, bB, mms, [hw2])
                    h_wr_reads[slot].append(hmmB)
                    hhe.append(bank_read(K, 'v', bB, [hsa], lambda v, hc=hc, b=bB: v.tensor_tensor(
                        hs_fm[:, hc * 512:(hc + 1) * 512], sA[:], b['ap'][:], ALU.mult)))
                for ttl in range(4):
                    tt = tb * 4 + ttl
                    for oc in range(2):
                        b = PB[4 + (ttl * 2 + oc) % 2]
                        mms = []
                        for kt in range(nhc):
                            mms.append(lambda t, st, sp, kt=kt, ttl=ttl, oc=oc, b=b: t.matmul(
                                b['ap'][:], hs_fm[:, kt * 512 + ttl * 128: kt * 512 + (ttl + 1) * 128],
                                w3b[:, kt * 1024 + oc * 512: kt * 1024 + (oc + 1) * 512], start=st, stop=sp))
                        hmm3 = mm_chain(K, b, mms, [h3, hhe[-1]] if (ttl == 0 and oc == 0) else [])
                        h_w3_reads.append(hmm3)
                        if wcol is None:
                            x_add(b, tt, oc)
                        else:
                            bank_read(K, 'v', b, [hwt[-1]], lambda v, tt=tt, oc=oc, b=b, e=e: v.scalar_tensor_tensor(
                                X[:, tt * 1024 + oc * 512: tt * 1024 + (oc + 1) * 512],
                                b['ap'][:], wtm[:, tt * 8 + e: tt * 8 + e + 1],
                                X[:, tt * 1024 + oc * 512: tt * 1024 + (oc + 1) * 512],
                                ALU.mult, ALU.add))

        ffn_pass("L2_Ws1", "L2_Ws2", "L2_Ws3", 11, hdeps=jn, w3kt0=0)
        ffn_pass("L2_Ws1", "L2_Ws2", "L2_Ws3", 11, hdeps=(), w3kt0=11)
        for e in range(8):
            ffn_pass("L2_W1", "L2_W2", "L2_W3", 11, e=e, wcol=True, w3kt0=0)
        fls.close()
        return K.snap()

    def emit_l3(join):
        ls = contextlib.ExitStack()
        mskd = K.sbuf("l3_mskd", [128, 128], BF16, ls)
        mskh = K.sbuf("l3_mskh", [128, 128], BF16, ls)
        h_m1 = K.op('y', join, lambda y: y.dma_start(mskd[:], W["L3_mask_diag"][:, :]), dma=True)
        h_m2 = K.op('y', join, lambda y: y.dma_start(mskh[:], W["L3_mask_head"][:, :]), dma=True)
        osc = contextlib.ExitStack()
        o_tm = K.sbuf("l3_otm", [128, NTT * 1024], BF16, osc)

        als = contextlib.ExitStack()
        vaug = K.sbuf("l3_vaug", [128, NTT * 130], BF16, als)
        h_vones = K.op('v', join, lambda v: v.memset(vaug[:], 1.0))
        wq_g = K.sbuf("l3_wqg", [128, 8 * 128], BF16, als)
        wk_g = K.sbuf("l3_wkg", [128, 8 * 128], BF16, als)
        wv_g = K.sbuf("l3_wvg", [128, 8 * 128], BF16, als)
        q_g = K.sbuf("l3_qg", [128, 2048], BF16, als)
        k_g = K.sbuf("l3_kg", [128, 2048], BF16, als)
        pt = K.sbuf("l3_pt", [128, 5 * 128], BF16, als)
        pts = K.sbuf("l3_pts", [128, 128], BF16, als)
        rvq = K.sbuf("l3_rv", [128, 1], F32, als)

        wld = [None]
        h_vaug_last = [h_vones]
        for hg in range(8):
            c0 = 128 * hg
            wdep = [] if wld[0] is None else [wld[0]]
            _, hq = load_wT(als, "L3_WqT", D, D, c0, 128, deps=join + wdep, dst=wq_g)
            _, hk = load_wT(als, "L3_WkT", D, D, c0, 128, deps=join + wdep, dst=wk_g)
            _, hv = load_wT(als, "L3_WvT", D, D, c0, 128, deps=join + wdep, dst=wv_g)
            for which, (wbuf, dst, hw) in enumerate([(wq_g, q_g, hq), (wk_g, k_g, hk)]):
                for tc in range(4):
                    b = PB[(which * 4 + tc) % 2]
                    mms = []
                    for kt in range(8):
                        mms.append(lambda t, st, sp, kt=kt, tc=tc, b=b, wbuf=wbuf: t.matmul(
                            b['ap'][:], wbuf[:, kt * 128:(kt + 1) * 128],
                            XNF[:, kt * 2048 + tc * 512: kt * 2048 + (tc + 1) * 512], start=st, stop=sp))
                    mm_chain(K, b, mms, [hw])
                    bank_read(K, 's', b, [], lambda s, tc=tc, b=b, dst=dst: s.copy(
                        dst[:, tc * 512:(tc + 1) * 512], b['ap'][:]))
            for tt in range(NTT):
                b = PB[4 + tt % 2]
                mms = []
                for kt in range(8):
                    mms.append(lambda t, st, sp, kt=kt, tt=tt, b=b: t.matmul(
                        b['ap'][:, 0:128], XNF[:, kt * 2048 + tt * 128: kt * 2048 + (tt + 1) * 128],
                        wv_g[:, kt * 128:(kt + 1) * 128], start=st, stop=sp))
                mm_chain(K, b, mms, [hv, h_vaug_last[0]] if tt == 0 else [hv])
                dst = vaug[:, tt * 130:(tt + 1) * 130].rearrange("p (h c) -> p h c", c=65)[:, :, 0:64]
                srcv = b['ap'][:, 0:128].rearrange("p (h c) -> p h c", c=64)
                h = bank_read(K, 'v', b, [], lambda v, dst=dst, srcv=srcv: v.tensor_copy(dst, srcv))
                h_vaug_last[0] = h
            h_pvp = [None]
            h_ttp = [None]
            for hl in range(2):
                head = 2 * hg + hl
                bse = 64 * hl
                for qt in range(NTT):
                    kts = list(range(max(0, qt - 4), qt + 1))
                    hexp = []
                    for ki, kt in enumerate(kts):
                        b = PB[ki % 3]
                        mm_chain(K, b, [lambda t, st, sp, kt=kt, qt=qt, b=b, bse=bse: t.matmul(
                            b['ap'][:, 0:128], k_g[bse:bse + 64, kt * 128:(kt + 1) * 128],
                            q_g[bse:bse + 64, qt * 128:(qt + 1) * 128],
                            start=True, stop=True, tile_position=(bse, 0))],
                            ([('s', K.n['s']), h_m1, h_m2] if (hl == 0 and qt == 0 and ki == 0) else []))
                        if kt == qt or (kt == qt - 4):
                            msk = mskd if kt == qt else mskh
                            hx = bank_read(K, 's', b, ([h_ttp[0]] if h_ttp[0] else []), lambda s, b=b: s.activation(
                                pts[:], b['ap'][:, 0:128], AF.Exp, scale=0.125))
                            htt = K.op('v', [hx] + ([h_pvp[0]] if h_pvp[0] else []), lambda v, ki=ki, msk=msk: v.tensor_tensor(
                                pt[:, ki * 128:(ki + 1) * 128], pts[:], msk[:], ALU.mult))
                            h_ttp[0] = htt
                            hexp.append(htt)
                        else:
                            hexp.append(bank_read(K, 's', b, [], lambda s, ki=ki, b=b: s.activation(
                                pt[:, ki * 128:(ki + 1) * 128], b['ap'][:, 0:128], AF.Exp, scale=0.125)))
                    bpv = PB[3]
                    h0 = None
                    for ki, kt in enumerate(kts):
                        d = (hexp + bpv['ready']) if ki == 0 else []
                        h0 = K.op('t', d, lambda t, ki=ki, kt=kt, hl=hl, bpv=bpv: t.matmul(
                            bpv['ap'][:, 0:65],
                            pt[:, ki * 128:(ki + 1) * 128],
                            vaug[:, kt * 130 + hl * 65: kt * 130 + (hl + 1) * 65],
                            start=(ki == 0), stop=(ki == len(kts) - 1)))
                    bpv['wrote'] = h0
                    bpv['ready'] = []
                    h_pvp[0] = h0
                    hrv = bank_read(K, 'v', bpv, [], lambda v, bpv=bpv: v.reciprocal(
                        rvq[:], bpv['ap'][:, 64:65]))
                    bank_read(K, 'v', bpv, [hrv], lambda v, qt=qt, head=head, bpv=bpv: v.tensor_scalar_mul(
                        o_tm[:, qt * 1024 + head * 64: qt * 1024 + head * 64 + 64],
                        bpv['ap'][:, 0:64], rvq[:]))
            wld[0] = ('t', K.n['t'])
        als.close()
        ols = contextlib.ExitStack()
        o_fm = K.sbuf("l3_ofm", [128, NDT * 2048], BF16, ols)
        wo = K.sbuf("l3_wo", [128, 8 * 512], BF16, ols)
        h_o_done = ('s', K.n['s'])
        hofm = []
        for dt in range(NDT):
            for g in range(2):
                b = QB[g]
                for j in range(8):
                    tt = g * 8 + j
                    d = (b['ready'] + [h_o_done]) if j == 0 else []
                    h = K.op('t', d, lambda t, tt=tt, dt=dt, b=b, j=j: t.transpose(
                        b['ap'][:, j * 128:(j + 1) * 128],
                        o_tm[:, tt * 1024 + dt * 128: tt * 1024 + (dt + 1) * 128], IDENT[:]))
                    if j == 7:
                        b['wrote'] = h
                        b['ready'] = []
                hofm.append(bank_read(K, 'v', b, [], lambda v, dt=dt, g=g, b=b: v.tensor_copy(
                    o_fm[:, dt * 2048 + g * 1024: dt * 2048 + (g + 1) * 1024], b['ap'][:])))
        for oc in range(2):
            _, hwo = load_wT(ols, "L3_WoT", D, D, c0=oc * 512, ncols=512,
                             deps=join + [('t', K.n['t'])], dst=wo)
            for tt in range(NTT):
                b = PB[(tt * 2 + oc) % 4]
                mms = []
                for kt in range(8):
                    mms.append(lambda t, st, sp, kt=kt, tt=tt, b=b: t.matmul(
                        b['ap'][:], o_fm[:, kt * 2048 + tt * 128: kt * 2048 + (tt + 1) * 128],
                        wo[:, kt * 512:(kt + 1) * 512], start=st, stop=sp))
                mm_chain(K, b, mms, [hwo] + (hofm[-1:] if (tt == 0 and oc == 0) else []))
                x_add(b, tt, oc)
        ols.close()
        osc.close()

        w1s = nc.dram_tensor("l3_w1s", [D, 4096], BF16)
        w2s = nc.dram_tensor("l3_w2s", [4096, D], BF16)
        pls = contextlib.ExitStack()
        jn = join + [('t', K.n['t']), ('v', K.n['v']), ('s', K.n['s']), ('dy', K.n['dy']), ('dg', K.n['dg'])]
        qstage = K.sbuf("l3_qst", [128, 4096], I32, pls)
        scb = K.sbuf("l3_scb", [128, 4096], F32, pls)
        wbf = K.sbuf("l3_wbf", [128, 4096], BF16, pls)
        scrow = K.sbuf("l3_scrow", [1, 4096], F32, pls)
        hq_prev = []
        for which, (qname, scname, nkt, ncols, sname) in enumerate([
                ("L3_fc1_q", "L3_fc1_scales", 8, 4096, w1s), ("L3_fc2_q", "L3_fc2_scales", 32, 1024, w2s)]):
            for kt in range(nkt):
                hqs = K.op('y', jn + hq_prev, lambda y, kt=kt, qname=qname, ncols=ncols: y.dma_start(
                    qstage[:, 0:ncols], W[qname][kt * 128:(kt + 1) * 128, :]), dma=True)
                hsr = K.op('y', [('t', K.n['t'])], lambda y, kt=kt, scname=scname, ncols=ncols: y.dma_start(
                    scrow[:, 0:ncols], W[scname][kt:kt + 1, :]), dma=True)
                hbc = []
                for c8 in range(ncols // 512):
                    b = PB[c8 % 2]
                    mm_chain(K, b, [lambda t, st, sp, kt=kt, c8=c8, b=b: t.matmul(
                        b['ap'][:], ONESR32[:, 0:128], scrow[:, c8 * 512:(c8 + 1) * 512],
                        start=True, stop=True)], [hsr])
                    hbc.append(bank_read(K, 'v', b, hq_prev, lambda v, c8=c8, b=b: v.tensor_copy(
                        scb[:, c8 * 512:(c8 + 1) * 512], b['ap'][:])))
                hdq = K.op('v', [hqs] + hbc, lambda v, ncols=ncols: v.scalar_tensor_tensor(
                    wbf[:, 0:ncols], qstage[:, 0:ncols], -128.0, scb[:, 0:ncols], ALU.add, ALU.mult))
                hst = K.op('y', [hdq], lambda y, kt=kt, sname=sname, ncols=ncols: y.dma_start(
                    sname[kt * 128:(kt + 1) * 128, :], wbf[:, 0:ncols]), dma=True)
                hq_prev = [hst]
        pls.close()

        fls = contextlib.ExitStack()
        jn2 = join + [('t', K.n['t']), ('v', K.n['v']), ('s', K.n['s']), ('dy', K.n['dy']), ('dg', K.n['dg'])]
        c_var = LN_ST[:, 48:64]
        c_r = LN_ST[:, 64:80]
        s2c = K.sbuf("l3_s2c", [128, 16], F32, fls)
        hv1 = K.op('v', jn2, lambda v: v.tensor_tensor(s2c[:], c_var[:], c_r[:], ALU.mult))
        hv2 = K.op('v', [hv1], lambda v: v.tensor_tensor(s2c[:], s2c[:], c_r[:], ALU.mult))
        hv3 = K.op('s', [hv2, h_cv], lambda s: s.activation(s2c[:], s2c[:], AF.Sqrt, bias=C_LNEPS))
        hv4 = K.op('v', [hv3], lambda v: v.reciprocal(s2c[:], s2c[:]))
        gmc, h_gm = load_cols(fls, "L3_gamma", D, jn2)
        btc, h_bt = load_cols(fls, "L3_beta", D, jn2)
        xn2t = K.sbuf("l3_xn2", [128, 1024], BF16, fls)
        xn3 = K.sbuf("l3_xn3", [128, 8 * 2048], BF16, fls)
        hx3 = []
        for tt in range(NTT):
            hsx = K.op('s', [hv4] + ([('t', K.n['t'])] if tt else []), lambda s, tt=tt: s.activation(
                xn2t[:], XNT[:, tt * 1024:(tt + 1) * 1024], AF.Identity, scale=s2c[:, tt:tt + 1]))
            for dt in range(8):
                qb = QB[dt % 2]
                hh = K.op('t', (qb['ready'] + [hsx]), lambda t, dt=dt, qb=qb: t.transpose(
                    qb['ap'][:, 0:128], xn2t[:, dt * 128:(dt + 1) * 128], IDENT[:]))
                qb['wrote'] = hh
                qb['ready'] = []
                hx3.append(bank_read(K, 's', qb, [h_gm, h_bt] if (tt == 0 and dt == 0) else [],
                                     lambda s, tt=tt, dt=dt, qb=qb: s.activation(
                    xn3[:, dt * 2048 + tt * 128: dt * 2048 + (tt + 1) * 128], qb['ap'][:, 0:128],
                    AF.Identity, scale=gmc[:, dt:dt + 1], bias=btc[:, dt:dt + 1])))
        b1c, h_b1 = load_cols(fls, "L3_fc1_bias", 4096, jn2)
        b2r, h_b2 = load_row(fls, "L3_fc2_bias", D, jn2, dt=BF16, cast=True)
        y_fm = K.sbuf("l3_yfm", [128, 32 * 512], BF16, fls)
        w1cb = K.sbuf("l3_w1cb", [128, 2 * 8 * 128], BF16, fls)
        w2cb = K.sbuf("l3_w2cb", [128, 2 * 1024], BF16, fls)
        h_w1_reads = [[], []]
        h_w2_reads = [[], []]
        for tb in range(4):
            hyd = []
            for fc in range(32):
                slot = fc % 2
                hw1 = K.op('g', h_w1_reads[slot] + (hq_prev if (tb == 0 and fc < 2) else []),
                           lambda g, fc=fc, slot=slot: g.dma_start(
                    w1cb[:, slot * 1024:(slot + 1) * 1024].rearrange("p (kt n) -> p kt n", n=128),
                    w1s.ap().rearrange("(kt p) n -> p kt n", p=128)[:, :, fc * 128:(fc + 1) * 128]), dma=True)
                h_w1_reads[slot] = []
                b = PB[slot]
                mms = []
                for kt in range(8):
                    mms.append(lambda t, st, sp, kt=kt, tb=tb, slot=slot, b=b: t.matmul(
                        b['ap'][:], w1cb[:, slot * 1024 + kt * 128: slot * 1024 + (kt + 1) * 128],
                        xn3[:, kt * 2048 + tb * 512: kt * 2048 + (tb + 1) * 512], start=st, stop=sp))
                hmm = mm_chain(K, b, mms, [hw1] + ([hx3[-1]] if (tb == 0 and fc == 0) else []))
                h_w1_reads[slot].append(hmm)
                hyd.append(bank_read(K, 's', b, [h_b1] if (tb == 0 and fc == 0) else [],
                                     lambda s, fc=fc, b=b: s.activation(
                    y_fm[:, fc * 512:(fc + 1) * 512], b['ap'][:], AF.Silu, bias=b1c[:, fc:fc + 1])))
            for tth in range(2):
                zb = [PB[i] for i in range(4)]
                for kt in range(32):
                    slot = kt % 2
                    hw2 = K.op('g', h_w2_reads[slot] + ([hq_prev[0]] if (tb == 0 and tth == 0 and kt < 2) else []),
                               lambda g, kt=kt, slot=slot: g.dma_start(
                        w2cb[:, slot * 1024:(slot + 1) * 1024],
                        w2s.ap()[kt * 128:(kt + 1) * 128, :]), dma=True)
                    h_w2_reads[slot] = []
                    for i2 in range(2):
                        ttl = tth * 2 + i2
                        for oc in range(2):
                            b = zb[i2 * 2 + oc]
                            d = [hw2, hyd[-1]] + (b['ready'] if kt == 0 else [])
                            hmm = K.op('t', d, lambda t, kt=kt, ttl=ttl, oc=oc, b=b, slot=slot, tb=tb: t.matmul(
                                b['ap'][:], y_fm[:, kt * 512 + ttl * 128: kt * 512 + (ttl + 1) * 128],
                                w2cb[:, slot * 1024 + oc * 512: slot * 1024 + (oc + 1) * 512],
                                start=(kt == 0), stop=False))
                            h_w2_reads[slot].append(hmm)
                            b['wrote'] = hmm
                            if kt == 0:
                                b['ready'] = []
                    if kt == 31:
                        for i2 in range(2):
                            ttl = tth * 2 + i2
                            for oc in range(2):
                                b = zb[i2 * 2 + oc]
                                hmm = K.op('t', [h_b2] if (i2 == 0 and oc == 0) else [],
                                           lambda t, oc=oc, b=b: t.matmul(
                                    b['ap'][:], ONESR[:, 0:128], b2r[:, oc * 512:(oc + 1) * 512],
                                    start=False, stop=True))
                                b['wrote'] = hmm
                for i2 in range(2):
                    ttl = tth * 2 + i2
                    tt = tb * 4 + ttl
                    for oc in range(2):
                        b = zb[i2 * 2 + oc]
                        h1_ = bank_read(K, 'v', b, [], lambda v, tt=tt, oc=oc, b=b: v.scalar_tensor_tensor(
                            X[:, tt * 1024 + oc * 512: tt * 1024 + (oc + 1) * 512],
                            b['ap'][:], 0.5,
                            X[:, tt * 1024 + oc * 512: tt * 1024 + (oc + 1) * 512], ALU.mult, ALU.add))
                        K.op('v', [h1_], lambda v, tt=tt, oc=oc: v.tensor_add(
                            X[:, tt * 1024 + oc * 512: tt * 1024 + (oc + 1) * 512],
                            X[:, tt * 1024 + oc * 512: tt * 1024 + (oc + 1) * 512],
                            XNT[:, tt * 1024 + oc * 512: tt * 1024 + (oc + 1) * 512]))
        fls.close()
        ls.close()
        return K.snap()

    join = [h_xload, h_ident, h_ones]
    emitted = []
    for li in sorted(layers):
        join = emit_ln(join)
        if li == 0:
            join = emit_l0(join)
        elif li == 1:
            join = emit_l1(join)
        elif li == 2:
            join = emit_l2(join)
        elif li == 3:
            join = emit_l3(join)
        if taps:
            tap = K.dram_out(f"tap_L{li}", [T, D], F32)
            ht = []
            for tt in range(NTT):
                ht.append(K.op('y', join if tt == 0 else [], lambda y, tt=tt, tap=tap: y.dma_start(
                    tap[tt * 128:(tt + 1) * 128, :], X[:, tt * 1024:(tt + 1) * 1024]), dma=True))
            K.wait_only('y', [ht[-1]])
        emitted.append(li)

    hout = []
    for tt in range(NTT):
        hout.append(K.op('y', join if tt == 0 else [], lambda y, tt=tt: y.dma_start(
            out[tt * 128:(tt + 1) * 128, :], X[:, tt * 1024:(tt + 1) * 1024]), dma=True))
    K.wait_only('y', [hout[-1]])
    return K



def prep_weights(inp, layers=(0, 1, 2, 3)):
    w = {}
    f32 = lambda a: np.ascontiguousarray(a, dtype=np.float32)
    w["ident"] = np.eye(128, dtype=ml_dtypes.bfloat16)
    w["ones_row"] = np.ones((1, 128), dtype=ml_dtypes.bfloat16)
    w["ones_row32"] = np.ones((1, 128), dtype=np.float32)
    cv = np.zeros((1, 16), dtype=np.float32)
    cv[0, 0] = 1e-5; cv[0, 1] = 1e-4; cv[0, 2] = 1e-8; cv[0, 3] = -9e28
    w["cvec"] = cv
    if 0 in layers:
        w["L0_WqT"] = f32(inp["L0_attn_Wq"].T)
        w["L0_WkT"] = f32(inp["L0_attn_Wk"].T)
        w["L0_WvT"] = f32(inp["L0_attn_Wv"].T)
        w["L0_WoT"] = f32(inp["L0_attn_Wo"].T)
        w["L0_bq"] = f32(inp["L0_attn_bq"])
        w["L0_bk"] = f32(inp["L0_attn_bk"])
        w["L0_bv"] = f32(inp["L0_attn_bv"])
        w["L0_bo"] = f32(inp["L0_attn_bo"])
        w["L0_J"] = f32(inp["L0_attn_J_attn"])
        w["L0_lam"] = f32(inp["L0_attn_lambda_attn"]).reshape(1, 1)
        w["L0_W1T"] = f32(inp["L0_ffn_W1"].T)
        w["L0_b1"] = f32(inp["L0_ffn_b1"])
        w["L0_W2T"] = f32(inp["L0_ffn_W2"].T)
        w["L0_b2"] = f32(inp["L0_ffn_b2"])
        w["L0_scale"] = f32(inp["L0_ffn_scale"]).reshape(1, 1)
    if 1 in layers:
        w["L1_Wq"] = f32(inp["L1_attn_Wq"])
        w["L1_Wk"] = f32(inp["L1_attn_Wk"])
        w["L1_Wv"] = f32(inp["L1_attn_Wv"])
        w["L1_Wo"] = f32(inp["L1_attn_Wo"])
        inv = 1.0 / 10000.0 ** (np.arange(0, 1024, 2, dtype=np.float64) / 1024.0)
        ang = (np.arange(2048, dtype=np.float64)[:, None] * inv[None, :])
        w["L1_cosq"] = np.cos(ang).astype(np.float32)
        w["L1_sinq"] = np.sin(ang).astype(np.float32)
        w["L1_cosk"] = np.ascontiguousarray(np.cos(ang[:, :32]).astype(np.float32))
        w["L1_sink"] = np.ascontiguousarray(np.sin(ang[:, :32]).astype(np.float32))
        w["L1_slot_init"] = f32(inp["L1_ffn_slot_init"])
        w["L1_slot_scale"] = f32(inp["L1_ffn_slot_scale"])
        w1T = f32(inp["L1_ffn_w1"].T)
        w["L1_w1aT"] = np.ascontiguousarray(w1T[0:1024])
        w["L1_w1bT"] = np.ascontiguousarray(w1T[1024:2048])
        w["L1_b1"] = f32(inp["L1_ffn_b1"])
        w["L1_w2T"] = f32(inp["L1_ffn_w2"].T)
        w["L1_b2"] = f32(inp["L1_ffn_b2"])
        w["L1_wihT"] = f32(inp["L1_ffn_gru_wih"].T)
        w["L1_whhT"] = f32(inp["L1_ffn_gru_whh"].T)
        w["L1_bih"] = f32(inp["L1_ffn_gru_bih"])
        w["L1_bhh"] = f32(inp["L1_ffn_gru_bhh"])
        w["L1_wvT"] = f32(inp["L1_ffn_wv"].T)
        w["L1_bv"] = f32(inp["L1_ffn_bv"])
        w["L1_woT"] = f32(inp["L1_ffn_wo"].T)
        w["L1_bo"] = f32(inp["L1_ffn_bo"])
        w["L1_tau"] = f32(inp["L1_ffn_tau"]).reshape(1, 1)
    if 2 in layers:
        w["L2_W_in"] = f32(inp["L2_attn_W_in"])
        w["L2_conv_w"] = f32(inp["L2_attn_conv_w"].reshape(2048, 4))
        w["L2_W_x"] = f32(inp["L2_attn_W_x"])
        w["L2_W_dt"] = f32(inp["L2_attn_W_dt"])
        w["L2_b_dt"] = f32(inp["L2_attn_b_dt"])
        w["L2_W_Bp"] = f32(inp["L2_attn_W_Bp"])
        w["L2_D"] = f32(inp["L2_attn_D_param"])
        w["L2_W_out"] = f32(inp["L2_attn_W_out"])
        w["L2_Wg"] = f32(inp["L2_ffn_Wg"])
        w["L2_W1"] = f32(inp["L2_ffn_W1"])
        w["L2_W2"] = f32(inp["L2_ffn_W2"])
        w["L2_W3"] = f32(inp["L2_ffn_W3"])
        w["L2_Ws1"] = f32(inp["L2_ffn_Ws1"])
        w["L2_Ws2"] = f32(inp["L2_ffn_Ws2"])
        w["L2_Ws3"] = f32(inp["L2_ffn_Ws3"])
    if 3 in layers:
        w["L3_WqT"] = f32(inp["L3_attn_Wq"].T)
        w["L3_WkT"] = f32(inp["L3_attn_Wk"].T)
        w["L3_WvT"] = f32(inp["L3_attn_Wv"].T)
        w["L3_WoT"] = f32(inp["L3_attn_Wo"].T)
        ii = np.arange(128)
        w["L3_mask_diag"] = (ii[:, None] <= ii[None, :]).astype(ml_dtypes.bfloat16)
        w["L3_mask_head"] = (ii[:, None] > ii[None, :]).astype(ml_dtypes.bfloat16)
        w["L3_gamma"] = f32(inp["L3_ffn_ln_gamma"])
        w["L3_beta"] = f32(inp["L3_ffn_ln_beta"])
        w["L3_fc1_q"] = np.ascontiguousarray(inp["L3_ffn_fc1_q"], dtype=np.int32)
        w["L3_fc1_scales"] = f32(inp["L3_ffn_fc1_scales"])
        w["L3_fc1_bias"] = f32(inp["L3_ffn_fc1_bias"])
        w["L3_fc2_q"] = np.ascontiguousarray(inp["L3_ffn_fc2_q"], dtype=np.int32)
        w["L3_fc2_scales"] = f32(inp["L3_ffn_fc2_scales"])
        w["L3_fc2_bias"] = f32(inp["L3_ffn_fc2_bias"])
    return w


def kernel(**inputs):
    layers = (0, 1, 2, 3)
    K = build_program(layers=layers)
    w = prep_weights(inputs, layers)
    x = np.ascontiguousarray(inputs["L0_attn_x"], dtype=np.float32)
    in_maps = []
    for c in range(8):
        m = dict(w)
        m["x"] = np.ascontiguousarray(x[c])
        in_maps.append(m)
    res = run_bass_kernel_spmd(K.nc, in_maps, core_ids=list(range(8)))
    return np.stack([res.results[c]["out"] for c in range(8)], axis=0)
